# version 08
# baseline (speedup 1.0000x reference)
import math
from contextlib import ExitStack

import numpy as np

import concourse.bass as bass
import concourse.mybir as mybir
from concourse.bass_utils import run_bass_kernel_spmd

F32 = mybir.dt.float32
AF = mybir.ActivationFunctionType
ALU = mybir.AluOpType
ET = mybir.EngineType
ds = bass.ds

SCALE_INV = 1.0 / math.sqrt(64 * 3)
LN_EPS = 1e-5



def idx_map(deltas):
    rel = np.asarray(deltas, dtype=np.int64)
    relc = np.clip(rel, -511, 511)
    sign = np.sign(relc)
    a = np.maximum(np.abs(relc), 1).astype(np.float64)
    log_base = math.log(511 / 128)
    log_pos = np.ceil(np.log(a / 128) / log_base * 127) + 128
    bucket = np.where(np.abs(relc) < 128, a, log_pos).astype(np.int64) * sign
    return np.clip(bucket + 256, 0, 511).astype(np.int64)


def iota_groups(vals):
    runs = []
    i = 0
    while i < len(vals):
        j = i
        while j < len(vals) and vals[j] == vals[i]:
            j += 1
        runs.append((int(vals[i]), j - i))
        i = j
    out = []
    pos = 0
    k = 0
    while k < len(runs):
        L = runs[k][1]
        m = k + 1
        step = None
        while m < len(runs) and runs[m][1] == L:
            st = runs[m][0] - runs[m - 1][0]
            if step is None:
                step = st
            elif st != step:
                break
            m += 1
        n = m - k
        out.append((pos, runs[k][0], 0 if step is None else step, L, n))
        pos += L * n
        k = m
    assert pos == len(vals)
    return out


class CFG:
    def __init__(self, S=8192):
        self.S = S
        self.D, self.H, self.DH, self.E, self.HF = 1024, 16, 64, 8, 4096
        self.T = S // 4
        self.NPAD = 1024
        self.NB = self.T + 2 * self.NPAD
        self.NKSNB = self.NB // 128
        fars = []
        for qi in range(4):
            q0 = qi * self.T
            fars.append(max(0, q0 - self.NPAD) +
                        max(0, S - q0 - self.T - self.NPAD))
        self.FB = max(max(fars), 512)
        assert self.FB % 512 == 0
        self.FB128 = self.FB // 128
        self.NS = self.NB + self.FB
        self.NKS = self.NS // 128
        assert self.NS % 512 == 0
        self.NMAC = self.NS // 512
        self.QBLK = min(1024, self.T)
        self.NQB = self.T // self.QBLK
        self.QC = min(512, self.QBLK)
        self.NQC = self.QBLK // self.QC
        self.UPAD = 893
        self.WU = 2 * self.UPAD + 1
        self.nearset = {}
        for a in range(self.T // 128):
            lo = hi = None
            for r in range(self.NKSNB):
                if self.cls(a, r) == "N":
                    if lo is None:
                        lo = r
                    hi = r
            assert lo is not None
            self.nearset[a] = (lo, hi)
        self.bk_lo = min(v[0] for v in self.nearset.values())
        self.bk_hi = max(v[1] for v in self.nearset.values())
        dN = np.arange(-self.UPAD, self.UPAD + 1)
        self.mapN = idx_map(dN)
        self.mapR = self.mapN[::-1].copy()
        NP128, T128, S128 = self.NPAD // 128, self.T // 128, S // 128
        for qi in range(4):
            padL = max(0, NP128 - T128 * qi)
            padR = max(0, T128 * qi + T128 + NP128 - S128)
            tail = self.FB128 - (S128 - (self.NKSNB - padL - padR))
            assert 0 <= padL <= 8 and 0 <= padR <= 8 and 0 <= tail <= 8, \
                (padL, padR, tail)

    def cls(self, a, r):
        dmin = 128 * (a - r) + self.NPAD - 127
        dmax = 128 * (a - r) + self.NPAD + 127
        if dmin >= 511:
            return "H"
        if dmax <= -511:
            return "L"
        return "N"



class Op:
    __slots__ = ("eng", "idx", "fn", "reads", "writes", "kind", "dyn",
                 "sem", "inc_amt", "inc_val", "deps", "pool_waits", "noinst")

    def __init__(self, eng, idx, fn, reads, writes, kind, dyn):
        self.eng, self.idx, self.fn = eng, idx, fn
        self.reads, self.writes = tuple(reads), tuple(writes)
        self.kind, self.dyn = kind, dyn
        self.sem = None
        self.inc_amt = 0
        self.inc_val = None
        self.deps = []
        self.pool_waits = []
        self.noinst = False


class Prog:
    ENGS = ("t", "v", "s", "y", "g")
    NPOOL = 16

    def __init__(self, nc, ctx):
        self.nc = nc
        self.ops = {e: [] for e in self.ENGS}
        self.bufw = {}
        self.bufr = {}
        self.sems = {}
        self.ecnt = {e: 0 for e in "tvsg"}
        for e in "tvsg":
            self.sems[("e", e)] = ctx.enter_context(nc.semaphore(f"es_{e}"))
        self.pool_i = {e: 0 for e in ("y", "s", "g")}
        self.pool_cnt = {}
        for e in ("y", "s", "g"):
            for i in range(self.NPOOL):
                self.sems[("p", e, i)] = ctx.enter_context(
                    nc.semaphore(f"dp_{e}{i}"))
                self.pool_cnt[("p", e, i)] = 0

    def op(self, eng, fn, reads=(), writes=(), kind="c", dyn=False):
        o = Op(eng, len(self.ops[eng]), fn, reads, writes, kind, dyn)
        deps = {}
        for b in o.reads:
            w = self.bufw.get(b)
            if w is not None:
                deps[(w.eng, w.kind, w.idx)] = w
        for b in o.writes:
            w = self.bufw.get(b)
            if w is not None:
                deps[(w.eng, w.kind, w.idx)] = w
            for r in self.bufr.get(b, ()):
                deps[(r.eng, r.kind, r.idx)] = r
        for d in deps.values():
            if d is o:
                continue
            if (d.eng == o.eng == "t" and d.kind == "c" and o.kind == "c"
                    and not d.dyn and not o.dyn):
                continue
            self._require_inc(d)
            o.deps.append(d)
        if o.kind in ("d", "g"):
            self._assign_pool(o)
        self.ops[eng].append(o)
        for b in o.reads:
            self.bufr.setdefault(b, []).append(o)
        for b in o.writes:
            self.bufw[b] = o
            self.bufr[b] = []
        return o

    def _require_inc(self, d):
        if d.inc_amt:
            return
        assert d.kind == "c", "async op without pool sem"
        d.sem = ("e", d.eng)
        d.inc_amt = 1

    def _assign_pool(self, o):
        e = o.eng
        assert e in ("y", "s", "g"), f"dma/ext op on engine {e}"
        amt = 16 if o.kind == "d" else 1
        i = self.pool_i[e]
        self.pool_i[e] = (i + 1) % self.NPOOL
        key = ("p", e, i)
        o.sem = key
        o.inc_amt = amt
        self.pool_cnt[key] += amt
        o.inc_val = self.pool_cnt[key]

    def barrier(self):
        marks = []
        for e in self.ENGS:
            for o in reversed(self.ops[e]):
                if o.kind == "c" and not o.noinst:
                    self._require_inc(o)
                    marks.append(o)
                    break
        pool_snap = [(k, v) for k, v in self.pool_cnt.items() if v > 0]
        for e in self.ENGS:
            def mk():
                def f(eng):
                    return None
                return f
            o = Op(e, len(self.ops[e]), mk(), (), (), "c", False)
            o.noinst = True
            o.deps = [m for m in marks if m.eng != e]
            o.pool_waits = list(pool_snap)
            self.ops[e].append(o)

    def final_waits(self):
        for k, v in list(self.pool_cnt.items()):
            if v > 0:
                def mk(k=k, v=v):
                    def f(eng):
                        eng.wait_ge(self.sems[k], v)
                        return None
                    return f
                o = Op("y", len(self.ops["y"]), mk(), (), (), "c", False)
                o.noinst = True
                self.ops["y"].append(o)

    def finalize(self, block):
        cnt = {e: 0 for e in self.ENGS}
        for e in self.ENGS:
            for o in self.ops[e]:
                if o.kind == "c" and o.inc_amt:
                    cnt[e] += 1
                    o.inc_val = cnt[e]
        plans = {}
        for e in self.ENGS:
            seq = []
            maxw = {}
            for o in self.ops[e]:
                waits = []
                for k, v in o.pool_waits:
                    if maxw.get(k, -1) < v:
                        waits.append((k, v))
                        maxw[k] = v
                if o.kind in ("d", "g"):
                    prev = o.inc_val - o.inc_amt
                    if prev > 0 and maxw.get(o.sem, -1) < prev:
                        waits.append((o.sem, prev))
                        maxw[o.sem] = prev
                for d in o.deps:
                    if maxw.get(d.sem, -1) < d.inc_val:
                        waits.append((d.sem, d.inc_val))
                        maxw[d.sem] = d.inc_val
                seq.append((o, waits))
            plans[e] = seq
        sems = self.sems

        def runner(e):
            def _run(eng):
                for o, waits in plans[e]:
                    for k, v in waits:
                        eng.wait_ge(sems[k], v)
                    inst = o.fn(eng)
                    if o.inc_val is not None:
                        assert inst is not None, f"op {o.eng}:{o.idx} no inst"
                        inst.then_inc(sems[o.sem], o.inc_amt)
            return _run

        block.tensor(runner("t"))
        block.vector(runner("v"))
        block.scalar(runner("s"))
        block.sync(runner("y"))
        block.gpsimd(runner("g"))

    def stats(self):
        return {e: len(self.ops[e]) for e in self.ENGS}


def _qi(nc, e):
    return nc.partition_id([e.engine]) % 4


def _rv(reg, lo, hi):
    from concourse.expressions_rust import RuntimeValue
    return RuntimeValue(reg, min_val=lo, max_val=hi)



def build(C, want_stats=False):
    nc = bass.Bass("TRN2", target_bir_lowering=False, enable_asserts=False)
    D, H, DH, E, HF = C.D, C.H, C.DH, C.E, C.HF
    T, NS, NB, WU, NPAD = C.T, C.NS, C.NB, C.WU, C.NPAD
    NKSNB, FB128, NKS = C.NKSNB, C.FB128, C.NKS
    QBLK, NQB, QC, NQC = C.QBLK, C.NQB, C.QC, C.NQC
    NP128, T128, S128 = NPAD // 128, T // 128, C.S // 128
    global NP128_, T128_, S128_, FB128_, NKSNB_
    NP128_, T128_, S128_, FB128_, NKSNB_ = NP128, T128, S128, C.FB128, C.NKSNB
    NUC = (WU + 511) // 512
    NKP = NKS // 2
    VSLOT = NKS + 24

    xin = nc.dram_tensor("xin", [NS, D], F32, kind="ExternalInput").ap()
    wq = nc.dram_tensor("wq", [D, D], F32, kind="ExternalInput").ap()
    wk = nc.dram_tensor("wk", [D, D], F32, kind="ExternalInput").ap()
    wv = nc.dram_tensor("wv", [D, D], F32, kind="ExternalInput").ap()
    bqv = nc.dram_tensor("bq", [1, D], F32, kind="ExternalInput").ap()
    bkv = nc.dram_tensor("bk", [1, D], F32, kind="ExternalInput").ap()
    bvv = nc.dram_tensor("bv", [1, D], F32, kind="ExternalInput").ap()
    rel = nc.dram_tensor("rel", [512, D], F32, kind="ExternalInput").ap()
    wrt = nc.dram_tensor("wr", [E, D], F32, kind="ExternalInput").ap()
    w1t = nc.dram_tensor("w1", [E, D, HF], F32, kind="ExternalInput").ap()
    b1t = nc.dram_tensor("b1", [E, HF], F32, kind="ExternalInput").ap()
    w2t = nc.dram_tensor("w2", [E, HF, D], F32, kind="ExternalInput").ap()
    b2t = nc.dram_tensor("b2", [E, D], F32, kind="ExternalInput").ap()
    outt = nc.dram_tensor("out", [T, D], F32, kind="ExternalOutput").ap()

    k_hbm = nc.dram_tensor("k_hbm", [D, NS], F32).ap()
    v_hbm = nc.dram_tensor("v_hbm", [H, NS, DH], F32).ap()
    q_hbm = nc.dram_tensor("q_hbm", [D, T], F32).ap()
    pkx_hbm = nc.dram_tensor("pkx_hbm", [D, WU], F32).ap()
    pqx_hbm = nc.dram_tensor("pqx_hbm", [D, WU], F32).ap()
    bqpr = nc.dram_tensor("bqpr", [QBLK, WU], F32).ap()
    NBKR = (C.bk_hi - C.bk_lo + 1) * 128
    bkpr = nc.dram_tensor("bkpr", [NBKR, WU], F32).ap()
    attn_hbm = nc.dram_tensor("attn_hbm", [D, T], F32).ap()

    with ExitStack() as ctx:
        P = Prog(nc, ctx)

        def sbuf(stack, name, shape):
            return stack.enter_context(nc.sbuf_tensor(name, shape, F32))

        def psum(stack, name, shape):
            return stack.enter_context(nc.psum_tensor(name, shape, F32))

        block = ctx.enter_context(nc.Block())

        regs = {}
        for ename, eng in (("v", nc.vector), ("s", nc.scalar), ("t", nc.tensor)):
            regs[ename] = {
                "padL": eng.alloc_register(f"padL_{ename}"),
                "padR": eng.alloc_register(f"padR_{ename}"),
                "tail": eng.alloc_register(f"tail_{ename}"),
                "nH": eng.alloc_register(f"nH_{ename}"),
                "sel": eng.alloc_register(f"sel_{ename}"),
                "zL": eng.alloc_register(f"zL_{ename}"),
                "zR": eng.alloc_register(f"zR_{ename}"),
                "zT": eng.alloc_register(f"zT_{ename}"),
            }
        NPr, Tr, Sr = None, None, None

        def emit_regsetup(ename):
            r = regs[ename]

            def f(e):
                qi = _qi(nc, e)
                e.reg_alu(r["padL"], NP128_ * 1 - T128_ * qi, 0, ALU.max)
                e.reg_alu(r["padR"], T128_ * qi + T128_ + NP128_ - S128_, 0,
                          ALU.max)
                e.reg_alu(r["tail"], r["padL"], r["padR"], ALU.add)
                e.reg_alu(r["tail"], r["tail"], -1, ALU.mult)
                e.reg_alu(r["tail"], r["tail"],
                          FB128_ - S128_ + NKSNB_, ALU.add)
                e.reg_alu(r["tail"], r["tail"], 0, ALU.max)
                e.reg_alu(r["zL"], r["padL"], 65, ALU.mult)
                e.reg_alu(r["zR"], r["padR"], -65, ALU.mult)
                e.reg_alu(r["zR"], r["zR"], (8 + NKSNB_) * 65, ALU.add)
                e.reg_alu(r["zT"], r["tail"], -65, ALU.mult)
                e.reg_alu(r["zT"], r["zT"], (16 + NKSNB_ + FB128_) * 65,
                          ALU.add)
                return e.reg_alu(r["nH"], T128_ * qi - NP128_, 0, ALU.max)
            return f

        ones_row = sbuf(ctx, "ones_row", [1, 512])
        ones_col = sbuf(ctx, "ones_col", [128, 1])
        cst = sbuf(ctx, "cst", [128, 2])
        ident = sbuf(ctx, "ident", [128, 128])
        P.op("v", emit_regsetup("v"), writes=["__regs_v"])
        P.op("s", emit_regsetup("s"), writes=["__regs_s"])
        P.op("t", emit_regsetup("t"), writes=["__regs_t"])
        RV_ZL = _rv(regs["v"]["zL"], 0, 8 * 65)
        RV_ZR = _rv(regs["v"]["zR"], NKSNB_ * 65, (8 + NKSNB_) * 65)
        RV_ZT = _rv(regs["v"]["zT"], (8 + NKSNB_ + FB128_) * 65,
                    (16 + NKSNB_ + FB128_) * 65)
        P.op("v", lambda e: e.memset(ones_row[:], 1.0), writes=["ones_row"])
        P.op("v", lambda e: e.memset(ones_col[:], 1.0), writes=["ones_col"])
        P.op("v", lambda e: e.memset(cst[:, 0:1], 1e-30), writes=["cst"])
        P.op("v", lambda e: e.memset(cst[:, 1:2], LN_EPS), writes=["cst"])
        P.op("g", lambda e: e.memset(ident[:], 1.0), writes=["ident"], kind="g")
        P.op("g", lambda e: e.affine_select(
            ident[:], ident[:], [[-1, 128]], ALU.is_equal, 0.0,
            base=0, channel_multiplier=1),
            reads=["ident"], writes=["ident"], kind="g")

        with ExitStack() as ph:
            wk_sb = sbuf(ph, "wk_sb", [128, 8 * D])
            wq_sb = sbuf(ph, "wq_sb", [128, 8 * D])
            bkc = sbuf(ph, "bkc", [128, 8])
            bqc = sbuf(ph, "bqc", [128, 8])

            P.op("y", lambda e: e.dma_start(out=wk_sb[:], in_=bass.AP(
                wk.tensor, 0, [[D, 128], [128 * D, 8], [1, D]])),
                reads=["@wk"], writes=["wk_sb"], kind="d")
            P.op("y", lambda e: e.dma_start(out=wq_sb[:], in_=bass.AP(
                wq.tensor, 0, [[D, 128], [128 * D, 8], [1, D]])),
                reads=["@wq"], writes=["wq_sb"], kind="d")
            P.op("y", lambda e: e.dma_start(
                out=bass.AP(bkc, 0, [[8, 128], [1, 8]]),
                in_=bass.AP(bkv.tensor, 0, [[1, 128], [128, 8]]),
                allow_slow_non_contiguous=True),
                reads=["@bk"], writes=["bkc"], kind="d")
            P.op("y", lambda e: e.dma_start(
                out=bass.AP(bqc, 0, [[8, 128], [1, 8]]),
                in_=bass.AP(bqv.tensor, 0, [[1, 128], [128, 8]]),
                allow_slow_non_contiguous=True),
                reads=["@bq"], writes=["bqc"], kind="d")

            with ExitStack() as pa:
                rel_tm = sbuf(pa, "rel_tm", [128, 4 * D])
                rel_fm = sbuf(pa, "rel_fm", [128, 8 * 512])
                pos_tm = sbuf(pa, "pos_tm", [128, 4 * D])
                gmap = sbuf(pa, "gmap", [128, WU])
                gcol = sbuf(pa, "gcol", [128, 1])
                gchunk = [sbuf(pa, f"gchunk{j}", [128, WU]) for j in range(4)]
                stA = sbuf(pa, "stA", [128, 512])
                brow = sbuf(pa, "brow", [1, D])
                p_trA = psum(pa, "p_trA", [128, 512])
                p_mmA = psum(pa, "p_mmA", [128, 512])

                P.op("y", lambda e: e.dma_start(out=rel_tm[:], in_=bass.AP(
                    rel.tensor, 0, [[D, 128], [128 * D, 4], [1, D]])),
                    reads=["@rel"], writes=["rel_tm"], kind="d")
                for fc in range(8):
                    for jc in range(4):
                        P.op("t", lambda e, fc=fc, jc=jc: e.transpose(
                            p_trA[:, 128 * jc:128 * (jc + 1)],
                            rel_tm[:, jc * D + fc * 128:jc * D + fc * 128 + 128],
                            ident[:]),
                            reads=["rel_tm", "ident"], writes=["p_trA"])
                    P.op("v", lambda e, fc=fc: e.tensor_copy(
                        rel_fm[:, fc * 512:(fc + 1) * 512], p_trA[:]),
                        reads=["p_trA"], writes=["rel_fm"])

                def pos_proj(wsb, wname, brsrc, brname):
                    for jc in range(4):
                        for half in range(2):
                            for fc in range(8):
                                P.op("t", lambda e, jc=jc, half=half, fc=fc,
                                     wsb=wsb: e.matmul(
                                         p_mmA[:],
                                         rel_fm[:, fc * 512 + jc * 128:
                                                fc * 512 + (jc + 1) * 128],
                                         wsb[:, fc * D + half * 512:
                                             fc * D + (half + 1) * 512],
                                         start=(fc == 0), stop=False),
                                     reads=["rel_fm", wname], writes=["p_mmA"])
                            P.op("t", lambda e, half=half, brsrc=brsrc:
                                 e.matmul(p_mmA[:], ones_row[:, 0:128],
                                          brsrc[:, half * 512:(half + 1) * 512],
                                          start=False, stop=True),
                                 reads=["ones_row", brname], writes=["p_mmA"])
                            P.op("v", lambda e, jc=jc, half=half: e.tensor_copy(
                                pos_tm[:, jc * D + half * 512:
                                       jc * D + (half + 1) * 512], p_mmA[:]),
                                reads=["p_mmA"], writes=["pos_tm"])

                def build_exp(map_arr, dst, dstname):
                    for (pos0, v0, vstep, L, nrun) in iota_groups(map_arr):
                        pat = [[vstep, nrun], [0, L]]
                        P.op("g", lambda e, pos0=pos0, v0=v0, pat=pat, L=L,
                             nrun=nrun: e.iota(
                                 gmap[:, pos0:pos0 + L * nrun], pat, base=v0,
                                 channel_multiplier=0,
                                 allow_small_or_imprecise_dtypes=True),
                             reads=["gmap"], writes=["gmap"], kind="g")
                    for jc in range(4):
                        P.op("g", lambda e, jc=jc: e.iota(
                            gcol[:], [[0, 1]], base=128 * jc,
                            channel_multiplier=1,
                            allow_small_or_imprecise_dtypes=True),
                            reads=["gcol"], writes=["gcol"], kind="g")
                        P.op("v", lambda e, jc=jc: e.tensor_scalar(
                            gchunk[jc][:], gmap[:], gcol[:], None, ALU.is_equal),
                            reads=["gmap", "gcol"], writes=[f"gchunk{jc}"])
                    for dc in range(8):
                        for uc in range(NUC):
                            u0, u1 = uc * 512, min(WU, (uc + 1) * 512)
                            for jc in range(4):
                                P.op("t", lambda e, jc=jc, dc=dc, u0=u0, u1=u1:
                                     e.matmul(
                                         p_mmA[:, 0:u1 - u0],
                                         pos_tm[:, jc * D + dc * 128:
                                                jc * D + (dc + 1) * 128],
                                         gchunk[jc][:, u0:u1],
                                         start=(jc == 0), stop=(jc == 3)),
                                     reads=["pos_tm", f"gchunk{jc}"],
                                     writes=["p_mmA"])
                            P.op("s", lambda e, u0=u0, u1=u1: e.copy(
                                stA[:, 0:u1 - u0], p_mmA[:, 0:u1 - u0]),
                                reads=["p_mmA"], writes=["stA"])
                            P.op("y", lambda e, dc=dc, u0=u0, u1=u1,
                                 dst=dst: e.dma_start(
                                     out=bass.AP(dst.tensor,
                                                 dc * 128 * WU + u0,
                                                 [[WU, 128], [1, u1 - u0]]),
                                     in_=stA[:, 0:u1 - u0]),
                                 reads=["stA"], writes=[dstname],
                                 kind="d")

                P.op("y", lambda e: e.dma_start(out=brow[:], in_=bkv),
                     reads=["@bk"], writes=["brow"], kind="d")
                pos_proj(wk_sb, "wk_sb", brow, "brow")
                build_exp(C.mapR, pkx_hbm, "@pkx")
                P.op("y", lambda e: e.dma_start(out=brow[:], in_=bqv),
                     reads=["@bq"], writes=["brow"], kind="d")
                pos_proj(wq_sb, "wq_sb", brow, "brow")
                build_exp(C.mapN, pqx_hbm, "@pqx")

                P.barrier()

            with ExitStack() as pb_:
                wv_sb = sbuf(pb_, "wv_sb", [128, 8 * D])
                bvrow = sbuf(pb_, "bvrow", [1, D])
                xt = [sbuf(pb_, f"xt{i}", [128, D]) for i in range(4)]
                xfm = [sbuf(pb_, f"xfm{i}", [128, 8 * 512]) for i in range(2)]
                stK = [sbuf(pb_, f"stK{i}", [128, 8 * 512]) for i in range(2)]
                stV = [sbuf(pb_, f"stV{i}", [128, D]) for i in range(2)]
                p_trB = psum(pb_, "p_trB", [128, 512])
                p_kB = [psum(pb_, f"p_kB{i}", [128, 512]) for i in range(2)]
                p_vB = psum(pb_, "p_vB", [128, D])

                P.op("y", lambda e: e.dma_start(out=wv_sb[:], in_=bass.AP(
                    wv.tensor, 0, [[D, 128], [128 * D, 8], [1, D]])),
                    reads=["@wv"], writes=["wv_sb"], kind="d")
                P.op("y", lambda e: e.dma_start(out=bvrow[:], in_=bvv),
                     reads=["@bv"], writes=["bvrow"], kind="d")

                lm0, lm1 = NPAD // 512, (NPAD + T) // 512
                for m in range(C.NMAC):
                    xf, xfn = xfm[m % 2], f"xfm{m % 2}"
                    for i in range(4):
                        P.op("s", lambda e, m=m, i=i, xt=xt: e.dma_start(
                            out=xt[i][:],
                            in_=xin[m * 512 + i * 128:m * 512 + (i + 1) * 128, :]),
                            reads=["@xin"], writes=[f"xt{i}"], kind="d")
                    for fc in range(8):
                        for i in range(4):
                            P.op("t", lambda e, fc=fc, i=i, xt=xt: e.transpose(
                                p_trB[:, i * 128:(i + 1) * 128],
                                xt[i][:, fc * 128:(fc + 1) * 128], ident[:]),
                                reads=[f"xt{i}", "ident"], writes=["p_trB"])
                        P.op("v", lambda e, fc=fc, xf=xf: e.tensor_copy(
                            xf[:, fc * 512:(fc + 1) * 512], p_trB[:]),
                            reads=["p_trB"], writes=[xfn])
                    projs = [(wk_sb, "wk_sb", bkc, "bkc", k_hbm, "@k_hbm",
                              m * 512, NS, 0)]
                    if lm0 <= m < lm1:
                        projs.append((wq_sb, "wq_sb", bqc, "bqc", q_hbm,
                                      "@q_hbm", (m - lm0) * 512, T, 1))
                    for (wsb, wname, bsb, bname, dhbm, dname, dcol, ncol, pi) \
                            in projs:
                        stx, stxn = stK[pi], f"stK{pi}"
                        for dc in range(8):
                            pk, pkn = p_kB[dc % 2], f"p_kB{dc % 2}"
                            for fc in range(8):
                                P.op("t", lambda e, dc=dc, fc=fc, wsb=wsb,
                                     xf=xf, pk=pk: e.matmul(
                                         pk[:],
                                         wsb[:, fc * D + dc * 128:
                                             fc * D + (dc + 1) * 128],
                                         xf[:, fc * 512:(fc + 1) * 512],
                                         start=(fc == 0), stop=(fc == 7)),
                                     reads=[wname, xfn], writes=[pkn])
                            P.op("v", lambda e, dc=dc, bsb=bsb, stx=stx, pk=pk:
                                 e.tensor_scalar(
                                     stx[:, dc * 512:(dc + 1) * 512], pk[:],
                                     bsb[:, dc:dc + 1], None, ALU.add),
                                 reads=[pkn, bname], writes=[stxn])
                        P.op("y", lambda e, dhbm=dhbm, dcol=dcol, ncol=ncol,
                             stx=stx: e.dma_start(
                                 out=bass.AP(dhbm.tensor, dcol,
                                             [[ncol, 128], [128 * ncol, 8],
                                              [1, 512]]),
                                 in_=bass.AP(stx, 0,
                                             [[4096, 128], [512, 8], [1, 512]])),
                             reads=[stxn], writes=[dname], kind="d")
                    for i in range(4):
                        sv, svn = stV[i % 2], f"stV{i % 2}"
                        for half in range(2):
                            for fc in range(8):
                                P.op("t", lambda e, i=i, half=half, fc=fc,
                                     xf=xf: e.matmul(
                                         p_vB[:, half * 512:(half + 1) * 512],
                                         xf[:, fc * 512 + i * 128:
                                            fc * 512 + (i + 1) * 128],
                                         wv_sb[:, fc * D + half * 512:
                                               fc * D + (half + 1) * 512],
                                         start=(fc == 0), stop=False),
                                     reads=[xfn, "wv_sb"], writes=["p_vB"])
                            P.op("t", lambda e, half=half: e.matmul(
                                p_vB[:, half * 512:(half + 1) * 512],
                                ones_row[:, 0:128],
                                bvrow[:, half * 512:(half + 1) * 512],
                                start=False, stop=True),
                                reads=["ones_row", "bvrow"], writes=["p_vB"])
                        P.op("v", lambda e, sv=sv: e.tensor_copy(sv[:], p_vB[:]),
                             reads=["p_vB"], writes=[svn])
                        P.op("y", lambda e, m=m, i=i, sv=sv: e.dma_start(
                            out=bass.AP(v_hbm.tensor, (m * 512 + i * 128) * DH,
                                        [[DH, 128], [NS * DH, H], [1, DH]]),
                            in_=bass.AP(sv, 0, [[D, 128], [DH, H], [1, DH]])),
                            reads=[svn], writes=["@v_hbm"], kind="d")

            P.barrier()


        NKP_NB = NKSNB // 2
        with ExitStack() as pc:
            k_sb = sbuf(pc, "k_sb", [128, NKP_NB * 128])
            k_fb = sbuf(pc, "k_fb", [67, C.FB])
            v_sb = sbuf(pc, "v_sb", [128, VSLOT * 65])
            q_sb = sbuf(pc, "q_sb", [128, T])
            q4_sb = sbuf(pc, "q4_sb", [67, QBLK])
            pkx_sb = sbuf(pc, "pkx_sb", [128, WU])
            pqx_sb = sbuf(pc, "pqx_sb", [128, WU])
            sat_q = sbuf(pc, "sat_q", [128, 2])
            sat_k = sbuf(pc, "sat_k", [128, 2])
            sq_scr = sbuf(pc, "sq_scr", [128, 512])
            nrm = sbuf(pc, "nrm", [1, 96])
            red = sbuf(pc, "red", [1, 16])
            negB = sbuf(pc, "negB", [128, 1])
            bias_nb = sbuf(pc, "bias_nb", [128, 2 * NKSNB])
            st_bq = [sbuf(pc, f"st_bq{i}", [128, 512]) for i in range(2)]
            st_bk = [sbuf(pc, f"st_bk{i}", [128, 512]) for i in range(2)]
            c2p_st = [sbuf(pc, f"c2p{i}", [128, 16 * 128]) for i in range(4)]
            p2c_st = [sbuf(pc, f"p2c{i}", [128, 512]) for i in range(2)]
            bias_sb = [sbuf(pc, f"bsb{i}", [128, 512]) for i in range(2)]
            pt_sb = [sbuf(pc, f"pt{i}", [128, 512]) for i in range(4)]
            w_sb = sbuf(pc, "w_sb", [1, 2 * QBLK])
            wbc = [sbuf(pc, f"wbc{i}", [65, 512]) for i in range(2)]
            ev1 = sbuf(pc, "ev1", [65, 512])
            ev2 = sbuf(pc, "ev2", [65, 512])
            rden = sbuf(pc, "rden", [1, 512])
            zrow = sbuf(pc, "zrow", [1, 512])
            fmask = sbuf(pc, "fmask", [2, C.FB])
            krowt = [sbuf(pc, f"krowt{i}", [1, 512]) for i in range(3)]
            qrowt = sbuf(pc, "qrowt", [1, 1024])
            nhcell = sbuf(pc, "nhcell", [1, 1])
            nhcell_i = pc.enter_context(
                nc.sbuf_tensor("nhcell_i", [1, 1], mybir.dt.int32))
            p_sc = [psum(pc, f"p_sc{i}", [128, 512]) for i in range(3)]
            p_ctx = psum(pc, "p_ctx", [65, 1536])
            p_g = [psum(pc, "p_g0", [128, 512])]
            p_bc = psum(pc, "p_bc", [128, 512])

            def pgb(i):
                return (p_g[0], "p_g0") if i % 2 == 0 else (p_bc, "p_bc")

            P.op("v", lambda e: e.memset(zrow[:], 0.0), writes=["zrow"])

            P.op("v", lambda e: e.store(nhcell_i[0:1, 0:1],
                                        _rv(regs["v"]["nH"], 0, 64) * 128),
                 writes=["nhcell_i"], dyn=True)
            P.op("v", lambda e: e.tensor_copy(nhcell[0:1, 0:1],
                                              nhcell_i[0:1, 0:1]),
                 reads=["nhcell_i"], writes=["nhcell"])
            P.op("g", lambda e: e.iota(
                fmask[:], [[1, C.FB]], base=0, channel_multiplier=0,
                allow_small_or_imprecise_dtypes=True),
                reads=["fmask"], writes=["fmask"], kind="g")
            P.op("t", lambda e: e.matmul(p_bc[0:2, 0:1], ones_row[0:1, 0:2],
                                         nhcell[0:1, 0:1],
                                         start=True, stop=True),
                 reads=["ones_row", "nhcell"], writes=["p_bc"])
            P.op("v", lambda e: e.tensor_copy(krowt[2][0:1, 0:1],
                                              p_bc[0:1, 0:1]),
                 reads=["p_bc"], writes=["krowt2"])
            P.op("v", lambda e: e.tensor_scalar(
                fmask[0:1, :], fmask[0:1, :], krowt[2][0:1, 0:1], None,
                ALU.is_ge),
                reads=["fmask", "krowt2"], writes=["fmask"])
            for cch in range((C.FB + 511) // 512):
                c0, c1 = cch * 512, min(C.FB, (cch + 1) * 512)
                P.op("s", lambda e, c0=c0, c1=c1: e.dma_start(
                    out=k_fb[65:66, c0:c1], in_=fmask[0:1, c0:c1]),
                    reads=["fmask"], writes=["k_fb_m"], kind="d")
                P.op("v", lambda e, c0=c0, c1=c1: e.tensor_scalar(
                    krowt[0][0:1, 0:c1 - c0], fmask[0:1, c0:c1], -1.0, 1.0,
                    ALU.mult, ALU.add),
                    reads=["fmask"], writes=["krowt0"])
                P.op("s", lambda e, c0=c0, c1=c1: e.dma_start(
                    out=k_fb[64:65, c0:c1], in_=krowt[0][0:1, 0:c1 - c0]),
                    reads=["krowt0"], writes=["k_fb_m"], kind="d")

            for cch in range((QBLK + 511) // 512):
                c0, c1 = cch * 512, min(QBLK, (cch + 1) * 512)
                P.op("s", lambda e, c0=c0, c1=c1: e.dma_start(
                    out=q4_sb[66:67, c0:c1], in_=ones_row[0:1, 0:c1 - c0]),
                    reads=["ones_row"], writes=["q4_sb"], kind="d")

            def vsl(kk):
                return 8 + kk if kk < NKSNB else 16 + kk

            for h in range(H):
                KW = NKP_NB * 128
                for h2 in range(2):
                    P.op("y", lambda e, h=h, h2=h2, KW=KW: e.dma_start(
                        out=bass.AP(k_sb, 64 * h2 * KW,
                                    [[KW, 64], [128, NKP_NB], [1, 128]]),
                        in_=bass.AP(k_hbm.tensor, 64 * h * NS + 128 * h2,
                                    [[NS, 64], [256, NKP_NB], [1, 128]])),
                        reads=["@k_hbm"], writes=["k_sb"], kind="d")
                P.op("y", lambda e, h=h: e.dma_start(
                    out=k_fb[0:64, :],
                    in_=bass.AP(k_hbm.tensor, 64 * h * NS + NB,
                                [[NS, 64], [1, C.FB]])),
                    reads=["@k_hbm"], writes=["k_fb"], kind="d")
                P.op("y", lambda e, h=h: e.dma_start(
                    out=bass.AP(v_sb, 8 * 65,
                                [[VSLOT * 65, 128], [65, NKSNB], [1, DH]]),
                    in_=bass.AP(v_hbm.tensor, h * NS * DH,
                                [[DH, 128], [128 * DH, NKSNB], [1, DH]])),
                    reads=["@v_hbm"], writes=["v_sb"], kind="d")
                P.op("y", lambda e, h=h: e.dma_start(
                    out=bass.AP(v_sb, (16 + NKSNB) * 65,
                                [[VSLOT * 65, 128], [65, FB128], [1, DH]]),
                    in_=bass.AP(v_hbm.tensor, (h * NS + NB) * DH,
                                [[DH, 128], [128 * DH, FB128], [1, DH]])),
                    reads=["@v_hbm"], writes=["v_sb"], kind="d")
                for hh in range(2):
                    P.op("y", lambda e, h=h, hh=hh: e.dma_start(
                        out=pkx_sb[64 * hh:64 * hh + 64, :],
                        in_=bass.AP(pkx_hbm.tensor, 64 * h * WU,
                                    [[WU, 64], [1, WU]])),
                        reads=["@pkx"], writes=["pkx_sb"], kind="d")
                    P.op("y", lambda e, h=h, hh=hh: e.dma_start(
                        out=pqx_sb[64 * hh:64 * hh + 64, :],
                        in_=bass.AP(pqx_hbm.tensor, 64 * h * WU,
                                    [[WU, 64], [1, WU]])),
                        reads=["@pqx"], writes=["pqx_sb"], kind="d")
                    P.op("y", lambda e, h=h, hh=hh: e.dma_start(
                        out=q_sb[64 * hh:64 * hh + 64, :],
                        in_=bass.AP(q_hbm.tensor, 64 * h * T,
                                    [[T, 64], [1, T]])),
                        reads=["@q_hbm"], writes=["q_sb"], kind="d")

                P.op("v", lambda e: e.memset(bass.AP(
                    v_sb, 8 * 65 + 64, [[VSLOT * 65, 128], [65, NKSNB], [1, 1]]),
                    1.0), reads=["v_sb"], writes=["v_sb"])
                P.op("v", lambda e: e.memset(bass.AP(
                    v_sb, (16 + NKSNB) * 65 + 64,
                    [[VSLOT * 65, 128], [65, FB128], [1, 1]]),
                    1.0), reads=["v_sb"], writes=["v_sb"])
                P.op("v", lambda e: e.memset(
                    v_sb[:, ds(RV_ZL, 8 * 65)], 0.0),
                    reads=["v_sb"], writes=["v_sb"], dyn=True)
                P.op("v", lambda e: e.memset(
                    v_sb[:, ds(RV_ZR, 8 * 65)], 0.0),
                    reads=["v_sb"], writes=["v_sb"], dyn=True)
                P.op("v", lambda e: e.memset(
                    v_sb[:, ds(RV_ZT, 8 * 65)], 0.0),
                    reads=["v_sb"], writes=["v_sb"], dyn=True)

                P.op("v", lambda e: e.tensor_copy(sat_q[:, 0:1],
                                                  pqx_sb[:, WU - 1:WU]),
                     reads=["pqx_sb"], writes=["sat_q"])
                P.op("v", lambda e: e.tensor_copy(sat_q[:, 1:2],
                                                  pqx_sb[:, 0:1]),
                     reads=["pqx_sb"], writes=["sat_q"])
                P.op("v", lambda e: e.tensor_copy(sat_k[0:64, 0:1],
                                                  pkx_sb[0:64, 0:1]),
                     reads=["pkx_sb"], writes=["sat_k"])
                P.op("v", lambda e: e.tensor_copy(sat_k[0:64, 1:2],
                                                  pkx_sb[0:64, WU - 1:WU]),
                     reads=["pkx_sb"], writes=["sat_k"])
                P.op("v", lambda e: e.tensor_copy(sat_k[64:128, 0:1],
                                                  pkx_sb[64:128, 0:1]),
                     reads=["pkx_sb"], writes=["sat_k"])
                P.op("v", lambda e: e.tensor_copy(sat_k[64:128, 1:2],
                                                  pkx_sb[64:128, WU - 1:WU]),
                     reads=["pkx_sb"], writes=["sat_k"])

                def norm_rows(src, srcname, plo, phi, n, dstoff):
                    nch = (n + 511) // 512
                    for cch in range(nch):
                        c0, c1 = cch * 512, min(n, (cch + 1) * 512)
                        pg, pgn = pgb(cch)
                        P.op("s", lambda e, c0=c0, c1=c1: e.square(
                            sq_scr[plo:phi, 0:c1 - c0], src[plo:phi, c0:c1]),
                            reads=[srcname], writes=["sq_scr"])
                        P.op("t", lambda e, c0=c0, c1=c1, pg=pg: e.matmul(
                            pg[0:1, 0:c1 - c0], ones_col[plo:phi, 0:1],
                            sq_scr[plo:phi, 0:c1 - c0], start=True,
                            stop=True),
                            reads=["sq_scr", "ones_col"], writes=[pgn])
                        P.op("v", lambda e, c0=c0, c1=c1, pg=pg, cch=cch:
                             e.reduce_max(
                                 nrm[0:1, dstoff + cch:dstoff + cch + 1],
                                 pg[0:1, 0:c1 - c0],
                                 axis=mybir.AxisListType.X),
                             reads=[pgn], writes=["nrm"])
                    return nch

                n1 = norm_rows(k_sb, "k_sb", 0, 64, NKP_NB * 128, 0)
                n2 = norm_rows(k_sb, "k_sb", 64, 128, NKP_NB * 128, n1)
                n3 = norm_rows(k_fb, "k_fb", 0, 64, C.FB, n1 + n2)
                P.op("v", lambda e, n=n1 + n2 + n3: e.reduce_max(
                    red[0:1, 0:1], nrm[0:1, 0:n],
                    axis=mybir.AxisListType.X), reads=["nrm"], writes=["red"])
                nq = norm_rows(q_sb, "q_sb", 0, 64, T, 0)
                P.op("v", lambda e, nq=nq: e.reduce_max(
                    red[0:1, 1:2], nrm[0:1, 0:nq], axis=mybir.AxisListType.X),
                    reads=["nrm"], writes=["red"])
                npk = norm_rows(pkx_sb, "pkx_sb", 0, 64, WU, 0)
                P.op("v", lambda e, npk=npk: e.reduce_max(
                    red[0:1, 2:3], nrm[0:1, 0:npk], axis=mybir.AxisListType.X),
                    reads=["nrm"], writes=["red"])
                npq = norm_rows(pqx_sb, "pqx_sb", 0, 64, WU, 0)
                P.op("v", lambda e, npq=npq: e.reduce_max(
                    red[0:1, 3:4], nrm[0:1, 0:npq], axis=mybir.AxisListType.X),
                    reads=["nrm"], writes=["red"])
                P.op("v", lambda e: e.tensor_mul(red[0:1, 4:5], red[0:1, 0:1],
                                                 red[0:1, 1:2]),
                     reads=["red"], writes=["red"])
                P.op("v", lambda e: e.tensor_mul(red[0:1, 5:6], red[0:1, 1:2],
                                                 red[0:1, 2:3]),
                     reads=["red"], writes=["red"])
                P.op("v", lambda e: e.tensor_mul(red[0:1, 6:7], red[0:1, 0:1],
                                                 red[0:1, 3:4]),
                     reads=["red"], writes=["red"])
                P.op("s", lambda e: e.activation(red[0:1, 7:10], red[0:1, 4:7],
                                                 AF.Ln, bias=cst[0:1, 0:1],
                                                 scale=1.0),
                     reads=["red", "cst"], writes=["red"])
                P.op("s", lambda e: e.activation(red[0:1, 7:10], red[0:1, 7:10],
                                                 AF.Exp, bias=0.0, scale=0.5),
                     reads=["red"], writes=["red"])
                P.op("v", lambda e: e.tensor_add(red[0:1, 10:11], red[0:1, 7:8],
                                                 red[0:1, 8:9]),
                     reads=["red"], writes=["red"])
                P.op("v", lambda e: e.tensor_add(red[0:1, 10:11],
                                                 red[0:1, 10:11],
                                                 red[0:1, 9:10]),
                     reads=["red"], writes=["red"])
                P.op("v", lambda e: e.tensor_scalar(
                    red[0:1, 11:12], red[0:1, 10:11],
                    -SCALE_INV * 1.02, -1.0, ALU.mult, ALU.add),
                    reads=["red"], writes=["red"])
                P.op("t", lambda e: e.matmul(p_bc[:, 0:1],
                                             ones_row[0:1, 0:128],
                                             red[0:1, 11:12],
                                             start=True, stop=True),
                     reads=["ones_row", "red"], writes=["p_bc"])
                P.op("v", lambda e: e.tensor_copy(negB[:], p_bc[:, 0:1]),
                     reads=["p_bc"], writes=["negB"])

                for kk in range(NKSNB):
                    half, pairc = kk % 2, (kk // 2) * 128
                    pg, pgn = pgb(kk)
                    P.op("t", lambda e, half=half, pairc=pairc, pg=pg: e.matmul(
                        pg[:, 0:2],
                        k_sb[64 * half:64 * half + 64, pairc:pairc + 128],
                        sat_q[64 * half:64 * half + 64, 0:2],
                        start=True, stop=True),
                        reads=["k_sb", "sat_q"], writes=[pgn])
                    P.op("v", lambda e, kk=kk, pg=pg: e.tensor_scalar(
                        bias_nb[:, 2 * kk:2 * kk + 2], pg[:, 0:2], SCALE_INV,
                        negB[:, 0:1], ALU.mult, ALU.add),
                        reads=[pgn, "negB"], writes=["bias_nb"])

                nfch = (C.FB + 511) // 512
                for cch in range(nfch):
                    c0, c1 = cch * 512, min(C.FB, (cch + 1) * 512)
                    for side in range(2):
                        pg, pgn = pgb(side)
                        P.op("t", lambda e, side=side, c0=c0, c1=c1, pg=pg:
                             e.matmul(
                                 pg[0:1, 0:c1 - c0],
                                 sat_q[0:64, side:side + 1],
                                 k_fb[0:64, c0:c1], start=True, stop=True),
                             reads=["sat_q", "k_fb", "k_fb_m"], writes=[pgn])
                        P.op("v", lambda e, side=side, c0=c0, c1=c1, pg=pg:
                             e.tensor_copy(krowt[side][0:1, 0:c1 - c0],
                                           pg[0:1, 0:c1 - c0]),
                             reads=[pgn], writes=[f"krowt{side}"])
                    P.op("v", lambda e, c0=c0, c1=c1: e.tensor_sub(
                        krowt[1][0:1, 0:c1 - c0], krowt[1][0:1, 0:c1 - c0],
                        krowt[0][0:1, 0:c1 - c0]),
                        reads=["krowt0", "krowt1"], writes=["krowt1"])
                    P.op("v", lambda e, c0=c0, c1=c1: e.tensor_mul(
                        krowt[1][0:1, 0:c1 - c0], krowt[1][0:1, 0:c1 - c0],
                        fmask[0:1, c0:c1]),
                        reads=["krowt1", "fmask"], writes=["krowt1"])
                    P.op("v", lambda e, c0=c0, c1=c1: e.tensor_add(
                        krowt[0][0:1, 0:c1 - c0], krowt[0][0:1, 0:c1 - c0],
                        krowt[1][0:1, 0:c1 - c0]),
                        reads=["krowt0", "krowt1"], writes=["krowt0"])
                    P.op("s", lambda e, c0=c0, c1=c1: e.dma_start(
                        out=k_fb[66:67, c0:c1], in_=krowt[0][0:1, 0:c1 - c0]),
                        reads=["krowt0"], writes=["k_fb_m"], kind="d")

                for qb in range(NQB):
                    q0b = qb * QBLK
                    for uc in range((QBLK + 511) // 512):
                        c0, c1 = uc * 512, min(QBLK, (uc + 1) * 512)
                        P.op("v", lambda e, c0=c0, c1=c1, q0b=q0b:
                             e.tensor_copy(q4_sb[0:64, c0:c1],
                                           q_sb[0:64, q0b + c0:q0b + c1]),
                             reads=["q_sb"], writes=["q4_sb"])
                        for side in range(2):
                            pg, pgn = pgb(side)
                            P.op("t", lambda e, c0=c0, c1=c1, q0b=q0b, pg=pg,
                                 side=side: e.matmul(
                                     pg[0:1, 0:c1 - c0],
                                     sat_k[64 * side:64 * side + 64,
                                           side:side + 1],
                                     q_sb[64 * side:64 * side + 64,
                                          q0b + c0:q0b + c1],
                                     start=True, stop=True),
                                 reads=["sat_k", "q_sb"], writes=[pgn])
                            P.op("v", lambda e, c0=c0, c1=c1, pg=pg,
                                 side=side: e.tensor_copy(
                                     qrowt[0:1,
                                           side * 512:side * 512 + c1 - c0],
                                     pg[0:1, 0:c1 - c0]),
                                 reads=[pgn], writes=["qrowt"])
                            P.op("s", lambda e, c0=c0, c1=c1, side=side:
                                 e.dma_start(
                                     out=q4_sb[64 + side:65 + side, c0:c1],
                                     in_=qrowt[0:1, side * 512:
                                               side * 512 + c1 - c0]),
                                 reads=["qrowt"], writes=["q4_sb"], kind="d")
                            P.op("s", lambda e, c0=c0, c1=c1, pg=pg, side=side:
                                 e.activation(
                                     w_sb[0:1, side * QBLK + c0:
                                          side * QBLK + c1],
                                     pg[0:1, 0:c1 - c0], AF.Exp,
                                     bias=0.0, scale=SCALE_INV),
                                 reads=[pgn], writes=["w_sb"])
                    for ac in range(QBLK // 128):
                        for uc in range(NUC):
                            u0, u1 = uc * 512, min(WU, (uc + 1) * 512)
                            hb = 64 * (uc % 2)
                            pg, pgn = pgb(uc)
                            stq, stqn = st_bq[uc % 2], f"st_bq{uc % 2}"
                            P.op("t", lambda e, ac=ac, u0=u0, u1=u1, q0b=q0b,
                                 pg=pg, hb=hb: e.matmul(
                                     pg[:, 0:u1 - u0],
                                     q_sb[hb:hb + 64,
                                          q0b + ac * 128:q0b + (ac + 1) * 128],
                                     pkx_sb[hb:hb + 64, u0:u1],
                                     start=True, stop=True),
                                 reads=["q_sb", "pkx_sb"], writes=[pgn])
                            P.op("v", lambda e, u0=u0, u1=u1, pg=pg, stq=stq:
                                 e.tensor_copy(stq[:, 0:u1 - u0],
                                               pg[:, 0:u1 - u0]),
                                 reads=[pgn], writes=[stqn])
                            P.op("y", lambda e, ac=ac, u0=u0, u1=u1, stq=stq:
                                 e.dma_start(
                                     out=bass.AP(bqpr.tensor,
                                                 ac * 128 * WU + u0,
                                                 [[WU, 128], [1, u1 - u0]]),
                                     in_=stq[:, 0:u1 - u0]),
                                 reads=[stqn], writes=["@bqpr"], kind="d")
                    if qb == 0:
                        for uc in range(NUC):
                            u0, u1 = uc * 512, min(WU, (uc + 1) * 512)
                            for r in range(C.bk_lo, C.bk_hi + 1):
                                half, pairc = r % 2, (r // 2) * 128
                                pg, pgn = pgb(r)
                                stk, stkn = st_bk[r % 2], f"st_bk{r % 2}"
                                P.op("t", lambda e, half=half, pairc=pairc,
                                     u0=u0, u1=u1, pg=pg: e.matmul(
                                         pg[:, 0:u1 - u0],
                                         k_sb[64 * half:64 * half + 64,
                                              pairc:pairc + 128],
                                         pqx_sb[64 * half:64 * half + 64,
                                                u0:u1],
                                         start=True, stop=True),
                                     reads=["k_sb", "pqx_sb"], writes=[pgn])
                                P.op("v", lambda e, u0=u0, u1=u1, pg=pg,
                                     stk=stk: e.tensor_copy(
                                         stk[:, 0:u1 - u0], pg[:, 0:u1 - u0]),
                                     reads=[pgn], writes=[stkn])
                                P.op("y", lambda e, r=r, u0=u0, u1=u1, stk=stk:
                                     e.dma_start(
                                         out=bass.AP(
                                             bkpr.tensor,
                                             (r - C.bk_lo) * 128 * WU + u0,
                                             [[WU, 128], [1, u1 - u0]]),
                                         in_=stk[:, 0:u1 - u0]),
                                     reads=[stkn], writes=["@bkpr"],
                                     kind="d")

                    for qc in range(NQC):
                        qcg = q0b + qc * QC
                        a_list = [qcg // 128 + i for i in range(QC // 128)]
                        for side in range(2):
                            P.op("t", lambda e, side=side, qc=qc, q0b=q0b:
                                 e.matmul(
                                     p_bc[0:65, 0:QC], ones_row[0:1, 0:65],
                                     w_sb[0:1, side * QBLK + qc * QC:
                                          side * QBLK + qc * QC + QC],
                                     start=True, stop=True),
                                 reads=["ones_row", "w_sb"], writes=["p_bc"])
                            P.op("v", lambda e, side=side: e.tensor_copy(
                                wbc[side][:, 0:QC], p_bc[0:65, 0:QC]),
                                reads=["p_bc"], writes=[f"wbc{side}"])
                        for side in range(3):
                            P.op("t", lambda e, side=side: e.matmul(
                                p_ctx[:, 512 * side:512 * side + QC],
                                zrow[0:1, 0:65], zrow[0:1, 0:QC],
                                start=True, stop=False),
                                reads=["zrow"], writes=["p_ctx"])
                        for ia, a in enumerate(a_list):
                            lo, hi = C.nearset[a]
                            nr = hi - lo + 1
                            aloc = a - q0b // 128
                            u0 = C.UPAD - 128 * (a - lo) - NPAD - 127
                            assert 0 <= u0 and u0 + 128 * (nr - 1) + 254 < WU
                            cs, csn = c2p_st[ia], f"c2p{ia}"
                            P.op("s", lambda e, aloc=aloc, u0=u0, nr=nr, cs=cs:
                                 e.dma_start(
                                     out=cs[:, 0:nr * 128],
                                     in_=bass.AP(bqpr.tensor,
                                                 aloc * 128 * WU + u0 + 127,
                                                 [[WU - 1, 128], [128, nr],
                                                  [1, 128]])),
                                 reads=["@bqpr"], writes=[csn], kind="d")
                        near_rs = {}
                        for ia, a in enumerate(a_list):
                            lo, hi = C.nearset[a]
                            for r in range(lo, hi + 1):
                                near_rs.setdefault(r, []).append(ia)
                        for r, ias in near_rs.items():
                            assert ias == list(range(ias[0], ias[0] + len(ias)))

                        def segs_of(kk):
                            segs = []
                            for ia, a in enumerate(a_list):
                                c = C.cls(a, kk)
                                if segs and segs[-1][2] == c:
                                    segs[-1][1] += 128
                                else:
                                    segs.append([ia * 128, 128, c])
                            return segs

                        def emit_qk(kk, qcg=qcg, qc=qc, a_list=a_list,
                                    near_rs=near_rs, segs_of=segs_of):
                            psc, pscn = p_sc[kk % 3], f"p_sc{kk % 3}"
                            ptb, ptbn = pt_sb[kk % 4], f"pt{kk % 4}"
                            if kk < NKSNB:
                                half, pairc = kk % 2, (kk // 2) * 128
                                P.op("t", lambda e, half=half, pairc=pairc,
                                     qcg=qcg, psc=psc: e.matmul(
                                         psc[:, 0:QC],
                                         k_sb[64 * half:64 * half + 64,
                                              pairc:pairc + 128],
                                         q_sb[64 * half:64 * half + 64,
                                              qcg:qcg + QC],
                                         start=True, stop=True),
                                     reads=["k_sb", "q_sb"], writes=[pscn])
                                segs = segs_of(kk)
                                nsegs = [s0 for s0 in segs if s0[2] == "N"]
                                if nsegs:
                                    (noff, nlen, _) = nsegs[0]
                                    ias = near_rs[kk]
                                    na = len(ias)
                                    a0 = a_list[ias[0]]
                                    u0p = 128 * (a0 - kk) + NPAD + C.UPAD - 127
                                    assert 0 <= u0p and \
                                        u0p + 128 * (na - 1) + 254 < WU
                                    pstj, pstjn = p2c_st[kk % 2], f"p2c{kk % 2}"
                                    P.op("s", lambda e, kk=kk, u0p=u0p, na=na,
                                         pstj=pstj: e.dma_start(
                                             out=pstj[:, 0:na * 128],
                                             in_=bass.AP(
                                                 bkpr.tensor,
                                                 (kk - C.bk_lo) * 128 * WU
                                                 + u0p + 127,
                                                 [[WU - 1, 128], [128, na],
                                                  [1, 128]])),
                                         reads=["@bkpr"], writes=[pstjn],
                                         kind="d")
                                    bsb, bsbn = bias_sb[kk % 2], f"bsb{kk % 2}"
                                    ptr, ptrn = pgb(kk)
                                    for j, ia in enumerate(ias):
                                        a = a_list[ia]
                                        lo, hi = C.nearset[a]
                                        cs, csn = c2p_st[ia], f"c2p{ia}"
                                        P.op("t", lambda e, j=j, kk=kk, lo=lo,
                                             cs=cs, ptr=ptr: e.transpose(
                                                 ptr[:, j * 128:(j + 1) * 128],
                                                 cs[:, (kk - lo) * 128:
                                                    (kk - lo + 1) * 128],
                                                 ident[:]),
                                             reads=[csn, "ident"],
                                             writes=[ptrn])
                                    P.op("v", lambda e, nlen=nlen, bsb=bsb,
                                         ptr=ptr, pstj=pstj: e.tensor_add(
                                             bsb[:, 0:nlen], ptr[:, 0:nlen],
                                             pstj[:, 0:nlen]),
                                         reads=[ptrn, pstjn], writes=[bsbn])
                                    P.op("v", lambda e, noff=noff, nlen=nlen,
                                         psc=psc, bsb=bsb: e.tensor_add(
                                             psc[:, noff:noff + nlen],
                                             psc[:, noff:noff + nlen],
                                             bsb[:, 0:nlen]),
                                         reads=[pscn, bsbn], writes=[pscn])
                                for (soff, slen, c) in segs:
                                    if c == "N":
                                        bcol = lambda e: negB[:, 0:1]
                                    elif c == "H":
                                        bcol = lambda e, kk=kk: bias_nb[
                                            :, 2 * kk:2 * kk + 1]
                                    else:
                                        bcol = lambda e, kk=kk: bias_nb[
                                            :, 2 * kk + 1:2 * kk + 2]
                                    P.op("s", lambda e, soff=soff, slen=slen,
                                         bcol=bcol, psc=psc, ptb=ptb:
                                         e.activation(
                                             ptb[:, soff:soff + slen],
                                             psc[:, soff:soff + slen], AF.Exp,
                                             bias=bcol(e), scale=SCALE_INV),
                                         reads=[pscn, "bias_nb", "negB"],
                                         writes=[ptbn])
                            else:
                                f = kk - NKSNB
                                P.op("t", lambda e, f=f, qc=qc, psc=psc:
                                     e.matmul(
                                         psc[:, 0:QC],
                                         k_fb[:, f * 128:(f + 1) * 128],
                                         q4_sb[:, qc * QC:qc * QC + QC],
                                         start=True, stop=True),
                                     reads=["k_fb", "k_fb_m", "q4_sb"],
                                     writes=[pscn])
                                P.op("s", lambda e, psc=psc, ptb=ptb:
                                     e.activation(
                                         ptb[:, 0:QC], psc[:, 0:QC], AF.Exp,
                                         bias=negB[:, 0:1], scale=SCALE_INV),
                                     reads=[pscn, "negB"], writes=[ptbn])

                        def emit_pv(kk, segs_of=segs_of):
                            ptb, ptbn = pt_sb[kk % 4], f"pt{kk % 4}"
                            slot = vsl(kk)
                            if kk < NKSNB:
                                for (soff, slen, c) in segs_of(kk):
                                    base = {"N": 0, "H": 512, "L": 1024}[c]
                                    P.op("t", lambda e, slot=slot, soff=soff,
                                         slen=slen, base=base, ptb=ptb:
                                         e.matmul(
                                             p_ctx[:, base + soff:
                                                   base + soff + slen],
                                             v_sb[:, slot * 65:slot * 65 + 65],
                                             ptb[:, soff:soff + slen],
                                             start=False, stop=False),
                                         reads=["v_sb", ptbn, "p_ctx"],
                                         writes=["p_ctx"])
                            else:
                                P.op("t", lambda e, slot=slot, ptb=ptb:
                                     e.matmul(
                                         p_ctx[:, 0:QC],
                                         v_sb[:, slot * 65:slot * 65 + 65],
                                         ptb[:, 0:QC],
                                         start=False, stop=False),
                                     reads=["v_sb", ptbn, "p_ctx"],
                                     writes=["p_ctx"])

                        pv_next = [0]

                        def pv_upto(k):
                            while pv_next[0] <= k:
                                emit_pv(pv_next[0])
                                pv_next[0] += 1

                        for m in range(0, NKSNB, 2):
                            emit_qk(m)
                            emit_qk(m + 1)
                            pv_upto(m - 1)
                        for kk in range(NKSNB, NKSNB + FB128):
                            emit_qk(kk)
                            pv_upto(kk - 2)
                        pv_upto(NKSNB + FB128 - 1)

                        for side in range(3):
                            P.op("t", lambda e, side=side: e.matmul(
                                p_ctx[:, 512 * side:512 * side + QC],
                                zrow[0:1, 0:65], zrow[0:1, 0:QC],
                                start=False, stop=True),
                                reads=["zrow", "p_ctx"], writes=["p_ctx"])

                        P.op("v", lambda e: e.tensor_mul(
                            ev1[:, 0:QC], wbc[0][:, 0:QC],
                            p_ctx[:, 512:512 + QC]),
                            reads=["wbc0", "p_ctx"], writes=["ev1"])
                        P.op("v", lambda e: e.tensor_mul(
                            ev2[:, 0:QC], wbc[1][:, 0:QC],
                            p_ctx[:, 1024:1024 + QC]),
                            reads=["wbc1", "p_ctx"], writes=["ev2"])
                        P.op("v", lambda e: e.tensor_add(
                            ev1[:, 0:QC], ev1[:, 0:QC], ev2[:, 0:QC]),
                            reads=["ev1", "ev2"], writes=["ev1"])
                        P.op("v", lambda e: e.tensor_add(
                            ev1[:, 0:QC], ev1[:, 0:QC], p_ctx[:, 0:QC]),
                            reads=["ev1", "p_ctx"], writes=["ev1"])
                        P.op("v", lambda e: e.reciprocal(
                            rden[0:1, 0:QC], ev1[64:65, 0:QC]),
                            reads=["ev1"], writes=["rden"])
                        P.op("t", lambda e: e.matmul(
                            p_bc[0:64, 0:QC], ones_row[0:1, 0:64],
                            rden[0:1, 0:QC], start=True, stop=True),
                            reads=["ones_row", "rden"], writes=["p_bc"])
                        P.op("v", lambda e: e.tensor_mul(
                            ev2[0:64, 0:QC], ev1[0:64, 0:QC],
                            p_bc[0:64, 0:QC]),
                            reads=["ev1", "p_bc"], writes=["ev2"])
                        P.op("y", lambda e, h=h, qcg=qcg: e.dma_start(
                            out=bass.AP(attn_hbm.tensor, 64 * h * T + qcg,
                                        [[T, 64], [1, QC]]),
                            in_=ev2[0:64, 0:QC]),
                            reads=["ev2"], writes=["@attn"], kind="d")

            P.barrier()

        with ExitStack() as pd:
            x1 = sbuf(pd, "x1", [128, 8 * T])
            y_acc = sbuf(pd, "y_acc", [128, T128 * D])
            xtd = [sbuf(pd, "xtD0", [128, D])]
            at_st = [sbuf(pd, "at_st0", [128, 512])]
            mrow = sbuf(pd, "mrow", [1, T])
            rrow = sbuf(pd, "rrow", [1, T])
            tr1 = sbuf(pd, "tr1", [1, 512])
            tmp_t = sbuf(pd, "tmp_t", [128, 512])
            w1g = [sbuf(pd, f"w1g{i}", [128, 8 * 256]) for i in range(2)]
            w2g = [sbuf(pd, f"w2g{i}", [128, 2 * D]) for i in range(2)]
            h_sb = sbuf(pd, "h_sb", [128, 6 * 512])
            b1c = sbuf(pd, "b1c", [128, E * (HF // 128)])
            b2r = sbuf(pd, "b2r", [E, D])
            wr_sb = sbuf(pd, "wr_sb", [128, 8 * E])
            probs = sbuf(pd, "probs", [128, 8])
            tmp8 = sbuf(pd, "tmp8", [128, 8])
            r1c = sbuf(pd, "r1c", [128, 1])
            r2c = sbuf(pd, "r2c", [128, 1])
            comb = sbuf(pd, "comb", [128, T128 * E])
            mcol = sbuf(pd, "mcol", [128, 1])
            rcol = sbuf(pd, "rcol", [128, 1])
            p_trD = psum(pd, "p_trD", [128, 512])
            p_mD = [psum(pd, f"p_mD{i}", [128, 512]) for i in range(2)]
            p_y = [psum(pd, f"p_y{i}", [128, D]) for i in range(2)]
            p_r = psum(pd, "p_r", [128, 512])

            for tc in range(T // 512):
                m = NPAD // 512 + tc
                for i in range(4):
                    xti, xtin = xtd[0], "xtD0"
                    P.op("s", lambda e, m=m, i=i, xti=xti: e.dma_start(
                        out=xti[:],
                        in_=xin[m * 512 + i * 128:m * 512 + (i + 1) * 128, :]),
                        reads=["@xin"], writes=[xtin], kind="d")
                    for fc in range(8):
                        P.op("t", lambda e, fc=fc, xti=xti, i=i: e.transpose(
                            p_trD[:, 0:128],
                            xti[:, fc * 128:(fc + 1) * 128], ident[:]),
                            reads=[xtin, "ident"], writes=["p_trD"])
                        P.op("v", lambda e, fc=fc, tc=tc, i=i: e.tensor_copy(
                            x1[:, fc * T + tc * 512 + i * 128:
                               fc * T + tc * 512 + (i + 1) * 128],
                            p_trD[:, 0:128]),
                            reads=["p_trD"], writes=["x1"])
            for fc in range(8):
                for tc in range(T // 512):
                    ai, ain = at_st[0], "at_st0"
                    P.op("y", lambda e, fc=fc, tc=tc, ai=ai: e.dma_start(
                        out=ai[:],
                        in_=bass.AP(attn_hbm.tensor, fc * 128 * T + tc * 512,
                                    [[T, 128], [1, 512]])),
                        reads=["@attn"], writes=[ain], kind="d")
                    P.op("v", lambda e, fc=fc, tc=tc, ai=ai: e.tensor_add(
                        x1[:, fc * T + tc * 512:fc * T + (tc + 1) * 512],
                        x1[:, fc * T + tc * 512:fc * T + (tc + 1) * 512],
                        ai[:]),
                        reads=["x1", ain], writes=["x1"])

            def ln_fm():
                for tc in range(T // 512):
                    pm, pmn = p_mD[tc % 2], f"p_mD{tc % 2}"
                    for fc in range(8):
                        P.op("t", lambda e, fc=fc, tc=tc, pm=pm: e.matmul(
                            pm[0:1, :], ones_col[:, 0:1],
                            x1[:, fc * T + tc * 512:fc * T + (tc + 1) * 512],
                            start=(fc == 0), stop=(fc == 7)),
                            reads=["x1", "ones_col"], writes=[pmn])
                    P.op("v", lambda e, tc=tc, pm=pm: e.tensor_scalar(
                        mrow[0:1, tc * 512:(tc + 1) * 512], pm[0:1, :],
                        1.0 / D, None, ALU.mult),
                        reads=[pmn], writes=["mrow"])
                for tc in range(T // 512):
                    pm, pmn = p_mD[tc % 2], f"p_mD{tc % 2}"
                    for fc in range(8):
                        P.op("s", lambda e, fc=fc, tc=tc: e.square(
                            tmp_t[:, 0:512],
                            x1[:, fc * T + tc * 512:fc * T + (tc + 1) * 512]),
                            reads=["x1"], writes=["tmp_t"])
                        P.op("t", lambda e, fc=fc, pm=pm: e.matmul(
                            pm[0:1, :], ones_col[:, 0:1], tmp_t[:, 0:512],
                            start=(fc == 0), stop=(fc == 7)),
                            reads=["tmp_t", "ones_col"], writes=[pmn])
                    P.op("v", lambda e, tc=tc, pm=pm: e.tensor_scalar(
                        rrow[0:1, tc * 512:(tc + 1) * 512], pm[0:1, :],
                        1.0 / D, None, ALU.mult),
                        reads=[pmn], writes=["rrow"])
                    P.op("v", lambda e, tc=tc: e.tensor_mul(
                        tr1[0:1, :], mrow[0:1, tc * 512:(tc + 1) * 512],
                        mrow[0:1, tc * 512:(tc + 1) * 512]),
                        reads=["mrow"], writes=["tr1"])
                    P.op("v", lambda e, tc=tc: e.tensor_sub(
                        rrow[0:1, tc * 512:(tc + 1) * 512],
                        rrow[0:1, tc * 512:(tc + 1) * 512], tr1[0:1, :]),
                        reads=["rrow", "tr1"], writes=["rrow"])
                P.op("s", lambda e: e.activation(rrow[0:1, 0:T],
                                                 rrow[0:1, 0:T], AF.Ln,
                                                 bias=cst[0:1, 1:2], scale=1.0),
                     reads=["rrow", "cst"], writes=["rrow"])
                P.op("s", lambda e: e.activation(rrow[0:1, 0:T],
                                                 rrow[0:1, 0:T], AF.Exp,
                                                 bias=0.0, scale=-0.5),
                     reads=["rrow"], writes=["rrow"])
                for tc in range(T // 512):
                    P.op("t", lambda e, tc=tc: e.matmul(
                        p_mD[0][:, :], ones_row[0:1, 0:128],
                        mrow[0:1, tc * 512:(tc + 1) * 512],
                        start=True, stop=True),
                        reads=["ones_row", "mrow"], writes=["p_mD0"])
                    P.op("t", lambda e, tc=tc: e.matmul(
                        p_mD[1][:, :], ones_row[0:1, 0:128],
                        rrow[0:1, tc * 512:(tc + 1) * 512],
                        start=True, stop=True),
                        reads=["ones_row", "rrow"], writes=["p_mD1"])
                    for fc in range(8):
                        P.op("v", lambda e, fc=fc, tc=tc: e.tensor_sub(
                            tmp_t[:, 0:512],
                            x1[:, fc * T + tc * 512:fc * T + (tc + 1) * 512],
                            p_mD[0][:, :]),
                            reads=["x1", "p_mD0"], writes=["tmp_t"])
                        P.op("v", lambda e, fc=fc, tc=tc: e.tensor_mul(
                            x1[:, fc * T + tc * 512:fc * T + (tc + 1) * 512],
                            tmp_t[:, 0:512], p_mD[1][:, :]),
                            reads=["tmp_t", "p_mD1"], writes=["x1"])

            ln_fm()

            for fc in range(8):
                P.op("y", lambda e, fc=fc: e.dma_start(
                    out=bass.AP(wr_sb, fc * E, [[8 * E, 128], [1, E]]),
                    in_=bass.AP(wrt.tensor, fc * 128, [[1, 128], [D, E]]),
                    allow_slow_non_contiguous=True),
                    reads=["@wr"], writes=["wr_sb"], kind="d")
            P.op("y", lambda e: e.dma_start(out=b2r[:], in_=b2t),
                 reads=["@b2"], writes=["b2r"], kind="d")
            for js in range(HF // 128):
                P.op("y", lambda e, js=js: e.dma_start(
                    out=bass.AP(b1c, js, [[E * (HF // 128), 128],
                                          [HF // 128, E]]),
                    in_=bass.AP(b1t.tensor, js * 128, [[1, 128], [HF, E]]),
                    allow_slow_non_contiguous=True),
                    reads=["@b1"], writes=["b1c"], kind="d")

            for tcb in range(T128):
                tc, i = tcb // 4, tcb % 4
                pm, pmn = p_mD[tcb % 2], f"p_mD{tcb % 2}"
                for fc in range(8):
                    P.op("t", lambda e, tcb=tcb, fc=fc, pm=pm: e.matmul(
                        pm[:, 0:E],
                        x1[:, fc * T + tcb * 128:fc * T + (tcb + 1) * 128],
                        wr_sb[:, fc * E:(fc + 1) * E],
                        start=(fc == 0), stop=(fc == 7)),
                        reads=["x1", "wr_sb"], writes=[pmn])
                P.op("v", lambda e, pm=pm: e.reduce_max(
                    r1c[:], pm[:, 0:E], axis=mybir.AxisListType.X,
                    negate=True),
                    reads=[pmn], writes=["r1c"])
                P.op("s", lambda e, pm=pm: e.activation(
                    probs[:], pm[:, 0:E], AF.Exp, bias=r1c[:], scale=1.0),
                    reads=[pmn, "r1c"], writes=["probs"])
                P.op("v", lambda e: e.reduce_sum(
                    r2c[:], probs[:], axis=mybir.AxisListType.X),
                    reads=["probs"], writes=["r2c"])
                P.op("v", lambda e: e.reciprocal(r1c[:], r2c[:]),
                     reads=["r2c"], writes=["r1c"])
                P.op("v", lambda e: e.tensor_scalar(
                    probs[:], probs[:], r1c[:], None, ALU.mult),
                    reads=["probs", "r1c"], writes=["probs"])
                P.op("v", lambda e: e.max(out=tmp8[:], in_=probs[:]),
                     reads=["probs"], writes=["tmp8"])
                P.op("v", lambda e, tcb=tcb: e.tensor_scalar(
                    comb[:, tcb * E:(tcb + 1) * E], probs[:],
                    tmp8[:, 1:2], None, ALU.is_ge),
                    reads=["probs", "tmp8"], writes=["comb"])
                P.op("v", lambda e, tcb=tcb: e.tensor_mul(
                    comb[:, tcb * E:(tcb + 1) * E],
                    comb[:, tcb * E:(tcb + 1) * E], probs[:]),
                    reads=["probs", "comb"], writes=["comb"])
                P.op("t", lambda e, tcb=tcb: e.transpose(
                    p_r[0:8, 0:128], comb[:, tcb * E:(tcb + 1) * E],
                    ident[:]),
                    reads=["comb", "ident"], writes=["p_r"])
                P.op("v", lambda e, tcb=tcb: e.tensor_copy(
                    h_sb[0:8, tcb * 128:(tcb + 1) * 128], p_r[0:8, 0:128]),
                    reads=["p_r"], writes=[f"h{0 if tcb < 8 else 1}"])

            for tcb in range(T128):
                for fc in range(8):
                    P.op("t", lambda e, tcb=tcb, fc=fc: e.transpose(
                        p_trD[:, 0:128],
                        x1[:, fc * T + tcb * 128:fc * T + (tcb + 1) * 128],
                        ident[:]),
                        reads=["x1", "ident"], writes=["p_trD"])
                    P.op("v", lambda e, tcb=tcb, fc=fc: e.tensor_copy(
                        y_acc[:, tcb * D + fc * 128:tcb * D + (fc + 1) * 128],
                        p_trD[:, 0:128]),
                        reads=["p_trD"], writes=["y_acc"])
                for half in range(2):
                    P.op("t", lambda e, tcb=tcb, half=half: e.matmul(
                        p_trD[:, 0:512],
                        h_sb[0:8, tcb * 128:(tcb + 1) * 128],
                        b2r[:, half * 512:(half + 1) * 512],
                        start=True, stop=True),
                        reads=[f"h{0 if tcb < 8 else 1}", "b2r"],
                        writes=["p_trD"])
                    P.op("v", lambda e, tcb=tcb, half=half: e.tensor_add(
                        y_acc[:, tcb * D + half * 512:
                              tcb * D + (half + 1) * 512],
                        y_acc[:, tcb * D + half * 512:
                              tcb * D + (half + 1) * 512],
                        p_trD[:, 0:512]),
                        reads=["y_acc", "p_trD"], writes=["y_acc"])

            NJG = HF // 256
            NTC = T // 512
            NSL = E * NJG
            G = NSL * NTC

            def slab_load(sl):
                ee, jg = sl // NJG, sl % NJG
                b = sl % 2
                wg1, wg1n = w1g[b], f"w1g{b}"
                wg2, wg2n = w2g[b], f"w2g{b}"
                P.op("y", lambda e, ee=ee, jg=jg, wg1=wg1: e.dma_start(
                    out=wg1[:],
                    in_=bass.AP(w1t.tensor, ee * D * HF + jg * 256,
                                [[HF, 128], [128 * HF, 8], [1, 256]])),
                    reads=["@w1"], writes=[wg1n], kind="d")
                P.op("y", lambda e, ee=ee, jg=jg, wg2=wg2: e.dma_start(
                    out=wg2[:],
                    in_=bass.AP(w2t.tensor, ee * HF * D + jg * 256 * D,
                                [[D, 128], [128 * D, 2], [1, D]])),
                    reads=["@w2"], writes=[wg2n], kind="d")

            def w1_group(g):
                sl, tc = g // NTC, g % NTC
                ee, jg = sl // NJG, sl % NJG
                wg1, wg1n = w1g[sl % 2], f"w1g{sl % 2}"
                hb, hn = (g % 3) * 1024, f"h{g % 3}"
                for js in range(2):
                    ph, phn = p_mD[js], f"p_mD{js}"
                    for fc in range(8):
                        P.op("t", lambda e, js=js, fc=fc, tc=tc,
                             wg1=wg1, ph=ph: e.matmul(
                                 ph[:],
                                 wg1[:, fc * 256 + js * 128:
                                     fc * 256 + (js + 1) * 128],
                                 x1[:, fc * T + tc * 512:
                                    fc * T + (tc + 1) * 512],
                                 start=(fc == 0), stop=(fc == 7)),
                             reads=[wg1n, "x1"], writes=[phn])
                    bcoli = ee * (HF // 128) + jg * 2 + js
                    P.op("s", lambda e, js=js, bcoli=bcoli, ph=ph, hb=hb,
                         hn=hn: e.activation(
                             h_sb[:, hb + js * 512:hb + (js + 1) * 512],
                             ph[:, 0:512], AF.Gelu,
                             bias=b1c[:, bcoli:bcoli + 1], scale=1.0),
                         reads=[phn, "b1c"], writes=[hn])

            def w2_group(g):
                sl, tc = g // NTC, g % NTC
                ee = sl // NJG
                wg2, wg2n = w2g[sl % 2], f"w2g{sl % 2}"
                hb, hn = (g % 3) * 1024, f"h{g % 3}"
                for i in range(4):
                    tcb = tc * 4 + i
                    py, pyn = p_y[i % 2], f"p_y{i % 2}"
                    for half in range(2):
                        for js in range(2):
                            P.op("t", lambda e, i=i, half=half, js=js,
                                 wg2=wg2, py=py, hb=hb: e.matmul(
                                     py[:, half * 512:half * 512 + 512],
                                     h_sb[:, hb + js * 512 + i * 128:
                                          hb + js * 512 + (i + 1) * 128],
                                     wg2[:, js * D + half * 512:
                                         js * D + (half + 1) * 512],
                                     start=(js == 0), stop=(js == 1)),
                                 reads=[hn, wg2n], writes=[pyn])
                    P.op("v", lambda e, ee=ee, tcb=tcb, py=py:
                         e.scalar_tensor_tensor(
                             y_acc[:, tcb * D:(tcb + 1) * D],
                             py[:, 0:D],
                             comb[:, tcb * E + ee:tcb * E + ee + 1],
                             y_acc[:, tcb * D:(tcb + 1) * D],
                             ALU.mult, ALU.add),
                         reads=[pyn, "comb", "y_acc"],
                         writes=["y_acc"])

            slab_load(0)
            w1_group(0)
            for g in range(G):
                sl, tc = g // NTC, g % NTC
                if tc == 0 and sl + 1 < NSL:
                    slab_load(sl + 1)
                if g + 1 < G:
                    w1_group(g + 1)
                w2_group(g)

            for tcb in range(T128):
                P.op("v", lambda e, tcb=tcb: e.reduce_sum(
                    mcol[:], y_acc[:, tcb * D:(tcb + 1) * D],
                    axis=mybir.AxisListType.X),
                    reads=["y_acc"], writes=["mcol"])
                P.op("v", lambda e: e.tensor_scalar(
                    mcol[:], mcol[:], 1.0 / D, None, ALU.mult),
                    reads=["mcol"], writes=["mcol"])
                P.op("s", lambda e, tcb=tcb: e.square(
                    h_sb[:, 0:D], y_acc[:, tcb * D:(tcb + 1) * D]),
                    reads=["y_acc"], writes=["h0"])
                P.op("v", lambda e: e.reduce_sum(
                    rcol[:], h_sb[:, 0:D], axis=mybir.AxisListType.X),
                    reads=["h0"], writes=["rcol"])
                P.op("v", lambda e: e.tensor_scalar(
                    rcol[:], rcol[:], 1.0 / D, None, ALU.mult),
                    reads=["rcol"], writes=["rcol"])
                P.op("v", lambda e: e.tensor_mul(h_sb[:, 0:1], mcol[:],
                                                 mcol[:]),
                     reads=["mcol"], writes=["h0"])
                P.op("v", lambda e: e.tensor_sub(rcol[:], rcol[:],
                                                 h_sb[:, 0:1]),
                     reads=["rcol", "h0"], writes=["rcol"])
                P.op("s", lambda e: e.activation(rcol[:], rcol[:], AF.Ln,
                                                 bias=cst[:, 1:2], scale=1.0),
                     reads=["rcol", "cst"], writes=["rcol"])
                P.op("s", lambda e: e.activation(rcol[:], rcol[:], AF.Exp,
                                                 bias=0.0, scale=-0.5),
                     reads=["rcol"], writes=["rcol"])
                P.op("v", lambda e, tcb=tcb: e.tensor_scalar(
                    h_sb[:, D:2 * D], y_acc[:, tcb * D:(tcb + 1) * D],
                    mcol[:], rcol[:], ALU.subtract, ALU.mult),
                    reads=["y_acc", "mcol", "rcol"], writes=["h1"])
                P.op("y", lambda e, tcb=tcb: e.dma_start(
                    out=outt[tcb * 128:(tcb + 1) * 128, :],
                    in_=h_sb[:, D:2 * D]),
                    reads=["h1"], writes=["@out"], kind="d")

        P.final_waits()
        stats = P.stats()
        P.finalize(block)
    if want_stats:
        return nc, stats
    return nc


def _hslice(h_sb, js, i):
    return h_sb[:, js * 512 + i * 128:js * 512 + (i + 1) * 128]



_NC_CACHE = {}


def kernel(L0_attn_hidden_states, L0_attn_rel_embeddings, L0_attn_Wq,
           L0_attn_bq, L0_attn_Wk, L0_attn_bk, L0_attn_Wv, L0_attn_bv,
           L0_ffn_Wr, L0_ffn_W1, L0_ffn_b1, L0_ffn_W2, L0_ffn_b2):
    x = np.ascontiguousarray(np.asarray(L0_attn_hidden_states, dtype=np.float32))
    B, S, D = x.shape
    C = CFG(S)
    key = S
    if key not in _NC_CACHE:
        _NC_CACHE[key] = build(C)
    nc = _NC_CACHE[key]

    def f32(a):
        return np.ascontiguousarray(np.asarray(a, dtype=np.float32))

    com = {
        "wq": f32(L0_attn_Wq), "wk": f32(L0_attn_Wk), "wv": f32(L0_attn_Wv),
        "bq": f32(L0_attn_bq).reshape(1, D),
        "bk": f32(L0_attn_bk).reshape(1, D),
        "bv": f32(L0_attn_bv).reshape(1, D),
        "rel": f32(L0_attn_rel_embeddings),
        "wr": f32(L0_ffn_Wr), "w1": f32(L0_ffn_W1), "b1": f32(L0_ffn_b1),
        "w2": f32(L0_ffn_W2), "b2": f32(L0_ffn_b2),
    }
    in_maps = []
    for core in range(8):
        b, qi = core // 4, core % 4
        q0 = qi * C.T
        nb = np.zeros((C.NB, D), dtype=np.float32)
        glo, ghi = q0 - C.NPAD, q0 + C.T + C.NPAD
        slo, shi = max(0, glo), min(S, ghi)
        nb[slo - glo:shi - glo] = x[b, slo:shi]
        farH = x[b, 0:max(0, q0 - C.NPAD)]
        farL = x[b, min(S, q0 + C.T + C.NPAD):S]
        fb = np.zeros((C.FB, D), dtype=np.float32)
        fb[0:len(farH)] = farH
        fb[len(farH):len(farH) + len(farL)] = farL
        ksrc = np.concatenate([nb, fb], axis=0)
        m = dict(com)
        m["xin"] = np.ascontiguousarray(ksrc)
        in_maps.append(m)

    res = run_bass_kernel_spmd(nc, in_maps, core_ids=list(range(8)))
    outf = np.empty((B, S, D), dtype=np.float32)
    for core in range(8):
        b, qi = core // 4, core % 4
        outf[b, qi * C.T:(qi + 1) * C.T] = res.results[core]["out"]
    return outf


# version 09 (final)
# speedup vs baseline: 1.0159x; 1.0159x over previous
import math
from contextlib import ExitStack

import numpy as np

import concourse.bass as bass
import concourse.mybir as mybir
from concourse.bass_utils import run_bass_kernel_spmd

F32 = mybir.dt.float32
AF = mybir.ActivationFunctionType
ALU = mybir.AluOpType
ET = mybir.EngineType
ds = bass.ds

SCALE_INV = 1.0 / math.sqrt(64 * 3)
LN_EPS = 1e-5



def idx_map(deltas):
    rel = np.asarray(deltas, dtype=np.int64)
    relc = np.clip(rel, -511, 511)
    sign = np.sign(relc)
    a = np.maximum(np.abs(relc), 1).astype(np.float64)
    log_base = math.log(511 / 128)
    log_pos = np.ceil(np.log(a / 128) / log_base * 127) + 128
    bucket = np.where(np.abs(relc) < 128, a, log_pos).astype(np.int64) * sign
    return np.clip(bucket + 256, 0, 511).astype(np.int64)


def iota_groups(vals):
    runs = []
    i = 0
    while i < len(vals):
        j = i
        while j < len(vals) and vals[j] == vals[i]:
            j += 1
        runs.append((int(vals[i]), j - i))
        i = j
    out = []
    pos = 0
    k = 0
    while k < len(runs):
        L = runs[k][1]
        m = k + 1
        step = None
        while m < len(runs) and runs[m][1] == L:
            st = runs[m][0] - runs[m - 1][0]
            if step is None:
                step = st
            elif st != step:
                break
            m += 1
        n = m - k
        out.append((pos, runs[k][0], 0 if step is None else step, L, n))
        pos += L * n
        k = m
    assert pos == len(vals)
    return out


class CFG:
    def __init__(self, S=8192):
        self.S = S
        self.D, self.H, self.DH, self.E, self.HF = 1024, 16, 64, 8, 4096
        self.T = S // 4
        self.NPAD = 1024
        self.NB = self.T + 2 * self.NPAD
        self.NKSNB = self.NB // 128
        fars = []
        for qi in range(4):
            q0 = qi * self.T
            fars.append(max(0, q0 - self.NPAD) +
                        max(0, S - q0 - self.T - self.NPAD))
        self.FB = max(max(fars), 512)
        assert self.FB % 512 == 0
        self.FB128 = self.FB // 128
        self.NS = self.NB + self.FB
        self.NKS = self.NS // 128
        assert self.NS % 512 == 0
        self.NMAC = self.NS // 512
        self.QBLK = min(1024, self.T)
        self.NQB = self.T // self.QBLK
        self.QC = min(512, self.QBLK)
        self.NQC = self.QBLK // self.QC
        self.UPAD = 893
        self.WU = 2 * self.UPAD + 1
        self.nearset = {}
        for a in range(self.T // 128):
            lo = hi = None
            for r in range(self.NKSNB):
                if self.cls(a, r) == "N":
                    if lo is None:
                        lo = r
                    hi = r
            assert lo is not None
            self.nearset[a] = (lo, hi)
        self.bk_lo = min(v[0] for v in self.nearset.values())
        self.bk_hi = max(v[1] for v in self.nearset.values())
        dN = np.arange(-self.UPAD, self.UPAD + 1)
        self.mapN = idx_map(dN)
        self.mapR = self.mapN[::-1].copy()
        NP128, T128, S128 = self.NPAD // 128, self.T // 128, S // 128
        for qi in range(4):
            padL = max(0, NP128 - T128 * qi)
            padR = max(0, T128 * qi + T128 + NP128 - S128)
            tail = self.FB128 - (S128 - (self.NKSNB - padL - padR))
            assert 0 <= padL <= 8 and 0 <= padR <= 8 and 0 <= tail <= 8, \
                (padL, padR, tail)

    def cls(self, a, r):
        dmin = 128 * (a - r) + self.NPAD - 127
        dmax = 128 * (a - r) + self.NPAD + 127
        if dmin >= 511:
            return "H"
        if dmax <= -511:
            return "L"
        return "N"



class Op:
    __slots__ = ("eng", "idx", "fn", "reads", "writes", "kind", "dyn",
                 "sem", "inc_amt", "inc_val", "deps", "pool_waits", "noinst")

    def __init__(self, eng, idx, fn, reads, writes, kind, dyn):
        self.eng, self.idx, self.fn = eng, idx, fn
        self.reads, self.writes = tuple(reads), tuple(writes)
        self.kind, self.dyn = kind, dyn
        self.sem = None
        self.inc_amt = 0
        self.inc_val = None
        self.deps = []
        self.pool_waits = []
        self.noinst = False


class Prog:
    ENGS = ("t", "v", "s", "y", "g")
    NPOOL = 16

    def __init__(self, nc, ctx):
        self.nc = nc
        self.ops = {e: [] for e in self.ENGS}
        self.bufw = {}
        self.bufr = {}
        self.sems = {}
        self.ecnt = {e: 0 for e in "tvsg"}
        for e in "tvsg":
            self.sems[("e", e)] = ctx.enter_context(nc.semaphore(f"es_{e}"))
        self.pool_i = {e: 0 for e in ("y", "s", "g")}
        self.pool_cnt = {}
        for e in ("y", "s", "g"):
            for i in range(self.NPOOL):
                self.sems[("p", e, i)] = ctx.enter_context(
                    nc.semaphore(f"dp_{e}{i}"))
                self.pool_cnt[("p", e, i)] = 0

    def op(self, eng, fn, reads=(), writes=(), kind="c", dyn=False):
        o = Op(eng, len(self.ops[eng]), fn, reads, writes, kind, dyn)
        deps = {}
        for b in o.reads:
            w = self.bufw.get(b)
            if w is not None:
                deps[(w.eng, w.kind, w.idx)] = w
        for b in o.writes:
            w = self.bufw.get(b)
            if w is not None:
                deps[(w.eng, w.kind, w.idx)] = w
            for r in self.bufr.get(b, ()):
                deps[(r.eng, r.kind, r.idx)] = r
        for d in deps.values():
            if d is o:
                continue
            if (d.eng == o.eng == "t" and d.kind == "c" and o.kind == "c"
                    and not d.dyn and not o.dyn):
                continue
            self._require_inc(d)
            o.deps.append(d)
        if o.kind in ("d", "g"):
            self._assign_pool(o)
        self.ops[eng].append(o)
        for b in o.reads:
            self.bufr.setdefault(b, []).append(o)
        for b in o.writes:
            self.bufw[b] = o
            self.bufr[b] = []
        return o

    def _require_inc(self, d):
        if d.inc_amt:
            return
        assert d.kind == "c", "async op without pool sem"
        d.sem = ("e", d.eng)
        d.inc_amt = 1

    def _assign_pool(self, o):
        e = o.eng
        assert e in ("y", "s", "g"), f"dma/ext op on engine {e}"
        amt = 16 if o.kind == "d" else 1
        i = self.pool_i[e]
        self.pool_i[e] = (i + 1) % self.NPOOL
        key = ("p", e, i)
        o.sem = key
        o.inc_amt = amt
        self.pool_cnt[key] += amt
        o.inc_val = self.pool_cnt[key]

    def barrier(self):
        marks = []
        for e in self.ENGS:
            for o in reversed(self.ops[e]):
                if o.kind == "c" and not o.noinst:
                    self._require_inc(o)
                    marks.append(o)
                    break
        pool_snap = [(k, v) for k, v in self.pool_cnt.items() if v > 0]
        for e in self.ENGS:
            def mk():
                def f(eng):
                    return None
                return f
            o = Op(e, len(self.ops[e]), mk(), (), (), "c", False)
            o.noinst = True
            o.deps = [m for m in marks if m.eng != e]
            o.pool_waits = list(pool_snap)
            self.ops[e].append(o)

    def final_waits(self):
        for k, v in list(self.pool_cnt.items()):
            if v > 0:
                def mk(k=k, v=v):
                    def f(eng):
                        eng.wait_ge(self.sems[k], v)
                        return None
                    return f
                o = Op("y", len(self.ops["y"]), mk(), (), (), "c", False)
                o.noinst = True
                self.ops["y"].append(o)

    def finalize(self, block):
        cnt = {e: 0 for e in self.ENGS}
        for e in self.ENGS:
            for o in self.ops[e]:
                if o.kind == "c" and o.inc_amt:
                    cnt[e] += 1
                    o.inc_val = cnt[e]
        plans = {}
        for e in self.ENGS:
            seq = []
            maxw = {}
            for o in self.ops[e]:
                waits = []
                for k, v in o.pool_waits:
                    if maxw.get(k, -1) < v:
                        waits.append((k, v))
                        maxw[k] = v
                if o.kind in ("d", "g"):
                    prev = o.inc_val - o.inc_amt
                    if prev > 0 and maxw.get(o.sem, -1) < prev:
                        waits.append((o.sem, prev))
                        maxw[o.sem] = prev
                for d in o.deps:
                    if maxw.get(d.sem, -1) < d.inc_val:
                        waits.append((d.sem, d.inc_val))
                        maxw[d.sem] = d.inc_val
                seq.append((o, waits))
            plans[e] = seq
        sems = self.sems

        def runner(e):
            def _run(eng):
                for o, waits in plans[e]:
                    for k, v in waits:
                        eng.wait_ge(sems[k], v)
                    inst = o.fn(eng)
                    if o.inc_val is not None:
                        assert inst is not None, f"op {o.eng}:{o.idx} no inst"
                        inst.then_inc(sems[o.sem], o.inc_amt)
            return _run

        block.tensor(runner("t"))
        block.vector(runner("v"))
        block.scalar(runner("s"))
        block.sync(runner("y"))
        block.gpsimd(runner("g"))

    def stats(self):
        return {e: len(self.ops[e]) for e in self.ENGS}


def _qi(nc, e):
    return nc.partition_id([e.engine]) % 4


def _rv(reg, lo, hi):
    from concourse.expressions_rust import RuntimeValue
    return RuntimeValue(reg, min_val=lo, max_val=hi)



def build(C, want_stats=False):
    nc = bass.Bass("TRN2", target_bir_lowering=False, enable_asserts=False)
    D, H, DH, E, HF = C.D, C.H, C.DH, C.E, C.HF
    T, NS, NB, WU, NPAD = C.T, C.NS, C.NB, C.WU, C.NPAD
    NKSNB, FB128, NKS = C.NKSNB, C.FB128, C.NKS
    QBLK, NQB, QC, NQC = C.QBLK, C.NQB, C.QC, C.NQC
    NP128, T128, S128 = NPAD // 128, T // 128, C.S // 128
    global NP128_, T128_, S128_, FB128_, NKSNB_
    NP128_, T128_, S128_, FB128_, NKSNB_ = NP128, T128, S128, C.FB128, C.NKSNB
    NUC = (WU + 511) // 512
    NKP = NKS // 2
    VSLOT = NKS + 24

    xin = nc.dram_tensor("xin", [NS, D], F32, kind="ExternalInput").ap()
    wq = nc.dram_tensor("wq", [D, D], F32, kind="ExternalInput").ap()
    wk = nc.dram_tensor("wk", [D, D], F32, kind="ExternalInput").ap()
    wv = nc.dram_tensor("wv", [D, D], F32, kind="ExternalInput").ap()
    bqv = nc.dram_tensor("bq", [1, D], F32, kind="ExternalInput").ap()
    bkv = nc.dram_tensor("bk", [1, D], F32, kind="ExternalInput").ap()
    bvv = nc.dram_tensor("bv", [1, D], F32, kind="ExternalInput").ap()
    rel = nc.dram_tensor("rel", [512, D], F32, kind="ExternalInput").ap()
    wrt = nc.dram_tensor("wr", [E, D], F32, kind="ExternalInput").ap()
    w1t = nc.dram_tensor("w1", [E, D, HF], F32, kind="ExternalInput").ap()
    b1t = nc.dram_tensor("b1", [E, HF], F32, kind="ExternalInput").ap()
    w2t = nc.dram_tensor("w2", [E, HF, D], F32, kind="ExternalInput").ap()
    b2t = nc.dram_tensor("b2", [E, D], F32, kind="ExternalInput").ap()
    outt = nc.dram_tensor("out", [T, D], F32, kind="ExternalOutput").ap()

    k_hbm = nc.dram_tensor("k_hbm", [D, NS], F32).ap()
    v_hbm = nc.dram_tensor("v_hbm", [H, NS, DH], F32).ap()
    q_hbm = nc.dram_tensor("q_hbm", [D, T], F32).ap()
    pkx_hbm = nc.dram_tensor("pkx_hbm", [D, WU], F32).ap()
    pqx_hbm = nc.dram_tensor("pqx_hbm", [D, WU], F32).ap()
    bqpr = nc.dram_tensor("bqpr", [QBLK, WU], F32).ap()
    NBKR = (C.bk_hi - C.bk_lo + 1) * 128
    bkpr = nc.dram_tensor("bkpr", [NBKR, WU], F32).ap()
    attn_hbm = nc.dram_tensor("attn_hbm", [D, T], F32).ap()

    with ExitStack() as ctx:
        P = Prog(nc, ctx)

        def sbuf(stack, name, shape):
            return stack.enter_context(nc.sbuf_tensor(name, shape, F32))

        def psum(stack, name, shape):
            return stack.enter_context(nc.psum_tensor(name, shape, F32))

        block = ctx.enter_context(nc.Block())

        regs = {}
        for ename, eng in (("v", nc.vector), ("s", nc.scalar), ("t", nc.tensor)):
            regs[ename] = {
                "padL": eng.alloc_register(f"padL_{ename}"),
                "padR": eng.alloc_register(f"padR_{ename}"),
                "tail": eng.alloc_register(f"tail_{ename}"),
                "nH": eng.alloc_register(f"nH_{ename}"),
                "sel": eng.alloc_register(f"sel_{ename}"),
                "zL": eng.alloc_register(f"zL_{ename}"),
                "zR": eng.alloc_register(f"zR_{ename}"),
                "zT": eng.alloc_register(f"zT_{ename}"),
            }
        NPr, Tr, Sr = None, None, None

        def emit_regsetup(ename):
            r = regs[ename]

            def f(e):
                qi = _qi(nc, e)
                e.reg_alu(r["padL"], NP128_ * 1 - T128_ * qi, 0, ALU.max)
                e.reg_alu(r["padR"], T128_ * qi + T128_ + NP128_ - S128_, 0,
                          ALU.max)
                e.reg_alu(r["tail"], r["padL"], r["padR"], ALU.add)
                e.reg_alu(r["tail"], r["tail"], -1, ALU.mult)
                e.reg_alu(r["tail"], r["tail"],
                          FB128_ - S128_ + NKSNB_, ALU.add)
                e.reg_alu(r["tail"], r["tail"], 0, ALU.max)
                e.reg_alu(r["zL"], r["padL"], 65, ALU.mult)
                e.reg_alu(r["zR"], r["padR"], -65, ALU.mult)
                e.reg_alu(r["zR"], r["zR"], (8 + NKSNB_) * 65, ALU.add)
                e.reg_alu(r["zT"], r["tail"], -65, ALU.mult)
                e.reg_alu(r["zT"], r["zT"], (16 + NKSNB_ + FB128_) * 65,
                          ALU.add)
                return e.reg_alu(r["nH"], T128_ * qi - NP128_, 0, ALU.max)
            return f

        ones_row = sbuf(ctx, "ones_row", [1, 512])
        ones_col = sbuf(ctx, "ones_col", [128, 1])
        cst = sbuf(ctx, "cst", [128, 2])
        ident = sbuf(ctx, "ident", [128, 128])
        P.op("v", emit_regsetup("v"), writes=["__regs_v"])
        P.op("s", emit_regsetup("s"), writes=["__regs_s"])
        P.op("t", emit_regsetup("t"), writes=["__regs_t"])
        RV_ZL = _rv(regs["v"]["zL"], 0, 8 * 65)
        RV_ZR = _rv(regs["v"]["zR"], NKSNB_ * 65, (8 + NKSNB_) * 65)
        RV_ZT = _rv(regs["v"]["zT"], (8 + NKSNB_ + FB128_) * 65,
                    (16 + NKSNB_ + FB128_) * 65)
        P.op("v", lambda e: e.memset(ones_row[:], 1.0), writes=["ones_row"])
        P.op("v", lambda e: e.memset(ones_col[:], 1.0), writes=["ones_col"])
        P.op("v", lambda e: e.memset(cst[:, 0:1], 1e-30), writes=["cst"])
        P.op("v", lambda e: e.memset(cst[:, 1:2], LN_EPS), writes=["cst"])
        P.op("g", lambda e: e.memset(ident[:], 1.0), writes=["ident"], kind="g")
        P.op("g", lambda e: e.affine_select(
            ident[:], ident[:], [[-1, 128]], ALU.is_equal, 0.0,
            base=0, channel_multiplier=1),
            reads=["ident"], writes=["ident"], kind="g")

        with ExitStack() as ph:
            wk_sb = sbuf(ph, "wk_sb", [128, 8 * D])
            wq_sb = sbuf(ph, "wq_sb", [128, 8 * D])
            bkc = sbuf(ph, "bkc", [128, 8])
            bqc = sbuf(ph, "bqc", [128, 8])

            P.op("y", lambda e: e.dma_start(out=wk_sb[:], in_=bass.AP(
                wk.tensor, 0, [[D, 128], [128 * D, 8], [1, D]])),
                reads=["@wk"], writes=["wk_sb"], kind="d")
            P.op("y", lambda e: e.dma_start(out=wq_sb[:], in_=bass.AP(
                wq.tensor, 0, [[D, 128], [128 * D, 8], [1, D]])),
                reads=["@wq"], writes=["wq_sb"], kind="d")
            P.op("y", lambda e: e.dma_start(
                out=bass.AP(bkc, 0, [[8, 128], [1, 8]]),
                in_=bass.AP(bkv.tensor, 0, [[1, 128], [128, 8]]),
                allow_slow_non_contiguous=True),
                reads=["@bk"], writes=["bkc"], kind="d")
            P.op("y", lambda e: e.dma_start(
                out=bass.AP(bqc, 0, [[8, 128], [1, 8]]),
                in_=bass.AP(bqv.tensor, 0, [[1, 128], [128, 8]]),
                allow_slow_non_contiguous=True),
                reads=["@bq"], writes=["bqc"], kind="d")

            with ExitStack() as pa:
                rel_tm = sbuf(pa, "rel_tm", [128, 4 * D])
                rel_fm = sbuf(pa, "rel_fm", [128, 8 * 512])
                pos_tm = sbuf(pa, "pos_tm", [128, 4 * D])
                gmap = sbuf(pa, "gmap", [128, WU])
                gcol = sbuf(pa, "gcol", [128, 1])
                gchunk = [sbuf(pa, f"gchunk{j}", [128, WU]) for j in range(4)]
                stA = sbuf(pa, "stA", [128, 512])
                brow = sbuf(pa, "brow", [1, D])
                p_trA = psum(pa, "p_trA", [128, 512])
                p_mmA = psum(pa, "p_mmA", [128, 512])

                P.op("y", lambda e: e.dma_start(out=rel_tm[:], in_=bass.AP(
                    rel.tensor, 0, [[D, 128], [128 * D, 4], [1, D]])),
                    reads=["@rel"], writes=["rel_tm"], kind="d")
                for fc in range(8):
                    for jc in range(4):
                        P.op("t", lambda e, fc=fc, jc=jc: e.transpose(
                            p_trA[:, 128 * jc:128 * (jc + 1)],
                            rel_tm[:, jc * D + fc * 128:jc * D + fc * 128 + 128],
                            ident[:]),
                            reads=["rel_tm", "ident"], writes=["p_trA"])
                    P.op("v", lambda e, fc=fc: e.tensor_copy(
                        rel_fm[:, fc * 512:(fc + 1) * 512], p_trA[:]),
                        reads=["p_trA"], writes=["rel_fm"])

                def pos_proj(wsb, wname, brsrc, brname):
                    for jc in range(4):
                        for half in range(2):
                            for fc in range(8):
                                P.op("t", lambda e, jc=jc, half=half, fc=fc,
                                     wsb=wsb: e.matmul(
                                         p_mmA[:],
                                         rel_fm[:, fc * 512 + jc * 128:
                                                fc * 512 + (jc + 1) * 128],
                                         wsb[:, fc * D + half * 512:
                                             fc * D + (half + 1) * 512],
                                         start=(fc == 0), stop=False),
                                     reads=["rel_fm", wname], writes=["p_mmA"])
                            P.op("t", lambda e, half=half, brsrc=brsrc:
                                 e.matmul(p_mmA[:], ones_row[:, 0:128],
                                          brsrc[:, half * 512:(half + 1) * 512],
                                          start=False, stop=True),
                                 reads=["ones_row", brname], writes=["p_mmA"])
                            P.op("v", lambda e, jc=jc, half=half: e.tensor_copy(
                                pos_tm[:, jc * D + half * 512:
                                       jc * D + (half + 1) * 512], p_mmA[:]),
                                reads=["p_mmA"], writes=["pos_tm"])

                def build_exp(map_arr, dst, dstname):
                    for (pos0, v0, vstep, L, nrun) in iota_groups(map_arr):
                        pat = [[vstep, nrun], [0, L]]
                        P.op("g", lambda e, pos0=pos0, v0=v0, pat=pat, L=L,
                             nrun=nrun: e.iota(
                                 gmap[:, pos0:pos0 + L * nrun], pat, base=v0,
                                 channel_multiplier=0,
                                 allow_small_or_imprecise_dtypes=True),
                             reads=["gmap"], writes=["gmap"], kind="g")
                    for jc in range(4):
                        P.op("g", lambda e, jc=jc: e.iota(
                            gcol[:], [[0, 1]], base=128 * jc,
                            channel_multiplier=1,
                            allow_small_or_imprecise_dtypes=True),
                            reads=["gcol"], writes=["gcol"], kind="g")
                        P.op("v", lambda e, jc=jc: e.tensor_scalar(
                            gchunk[jc][:], gmap[:], gcol[:], None, ALU.is_equal),
                            reads=["gmap", "gcol"], writes=[f"gchunk{jc}"])
                    for dc in range(8):
                        for uc in range(NUC):
                            u0, u1 = uc * 512, min(WU, (uc + 1) * 512)
                            for jc in range(4):
                                P.op("t", lambda e, jc=jc, dc=dc, u0=u0, u1=u1:
                                     e.matmul(
                                         p_mmA[:, 0:u1 - u0],
                                         pos_tm[:, jc * D + dc * 128:
                                                jc * D + (dc + 1) * 128],
                                         gchunk[jc][:, u0:u1],
                                         start=(jc == 0), stop=(jc == 3)),
                                     reads=["pos_tm", f"gchunk{jc}"],
                                     writes=["p_mmA"])
                            P.op("s", lambda e, u0=u0, u1=u1: e.copy(
                                stA[:, 0:u1 - u0], p_mmA[:, 0:u1 - u0]),
                                reads=["p_mmA"], writes=["stA"])
                            P.op("y", lambda e, dc=dc, u0=u0, u1=u1,
                                 dst=dst: e.dma_start(
                                     out=bass.AP(dst.tensor,
                                                 dc * 128 * WU + u0,
                                                 [[WU, 128], [1, u1 - u0]]),
                                     in_=stA[:, 0:u1 - u0]),
                                 reads=["stA"], writes=[dstname],
                                 kind="d")

                P.op("y", lambda e: e.dma_start(out=brow[:], in_=bkv),
                     reads=["@bk"], writes=["brow"], kind="d")
                pos_proj(wk_sb, "wk_sb", brow, "brow")
                build_exp(C.mapR, pkx_hbm, "@pkx")
                P.op("y", lambda e: e.dma_start(out=brow[:], in_=bqv),
                     reads=["@bq"], writes=["brow"], kind="d")
                pos_proj(wq_sb, "wq_sb", brow, "brow")
                build_exp(C.mapN, pqx_hbm, "@pqx")

                P.barrier()

            with ExitStack() as pb_:
                wv_sb = sbuf(pb_, "wv_sb", [128, 8 * D])
                bvrow = sbuf(pb_, "bvrow", [1, D])
                xt = [sbuf(pb_, f"xt{i}", [128, D]) for i in range(4)]
                xfm = [sbuf(pb_, f"xfm{i}", [128, 8 * 512]) for i in range(2)]
                stK = [sbuf(pb_, f"stK{i}", [128, 8 * 512]) for i in range(2)]
                stV = [sbuf(pb_, f"stV{i}", [128, D]) for i in range(2)]
                p_trB = psum(pb_, "p_trB", [128, 512])
                p_kB = [psum(pb_, f"p_kB{i}", [128, 512]) for i in range(2)]
                p_vB = psum(pb_, "p_vB", [128, D])

                P.op("y", lambda e: e.dma_start(out=wv_sb[:], in_=bass.AP(
                    wv.tensor, 0, [[D, 128], [128 * D, 8], [1, D]])),
                    reads=["@wv"], writes=["wv_sb"], kind="d")
                P.op("y", lambda e: e.dma_start(out=bvrow[:], in_=bvv),
                     reads=["@bv"], writes=["bvrow"], kind="d")

                lm0, lm1 = NPAD // 512, (NPAD + T) // 512
                for m in range(C.NMAC):
                    xf, xfn = xfm[m % 2], f"xfm{m % 2}"
                    for i in range(4):
                        P.op("s", lambda e, m=m, i=i, xt=xt: e.dma_start(
                            out=xt[i][:],
                            in_=xin[m * 512 + i * 128:m * 512 + (i + 1) * 128, :]),
                            reads=["@xin"], writes=[f"xt{i}"], kind="d")
                    for fc in range(8):
                        for i in range(4):
                            P.op("t", lambda e, fc=fc, i=i, xt=xt: e.transpose(
                                p_trB[:, i * 128:(i + 1) * 128],
                                xt[i][:, fc * 128:(fc + 1) * 128], ident[:]),
                                reads=[f"xt{i}", "ident"], writes=["p_trB"])
                        P.op("v", lambda e, fc=fc, xf=xf: e.tensor_copy(
                            xf[:, fc * 512:(fc + 1) * 512], p_trB[:]),
                            reads=["p_trB"], writes=[xfn])
                    projs = [(wk_sb, "wk_sb", bkc, "bkc", k_hbm, "@k_hbm",
                              m * 512, NS, 0)]
                    if lm0 <= m < lm1:
                        projs.append((wq_sb, "wq_sb", bqc, "bqc", q_hbm,
                                      "@q_hbm", (m - lm0) * 512, T, 1))
                    for (wsb, wname, bsb, bname, dhbm, dname, dcol, ncol, pi) \
                            in projs:
                        stx, stxn = stK[pi], f"stK{pi}"
                        for dc in range(8):
                            pk, pkn = p_kB[dc % 2], f"p_kB{dc % 2}"
                            for fc in range(8):
                                P.op("t", lambda e, dc=dc, fc=fc, wsb=wsb,
                                     xf=xf, pk=pk: e.matmul(
                                         pk[:],
                                         wsb[:, fc * D + dc * 128:
                                             fc * D + (dc + 1) * 128],
                                         xf[:, fc * 512:(fc + 1) * 512],
                                         start=(fc == 0), stop=(fc == 7)),
                                     reads=[wname, xfn], writes=[pkn])
                            P.op("v", lambda e, dc=dc, bsb=bsb, stx=stx, pk=pk:
                                 e.tensor_scalar(
                                     stx[:, dc * 512:(dc + 1) * 512], pk[:],
                                     bsb[:, dc:dc + 1], None, ALU.add),
                                 reads=[pkn, bname], writes=[stxn])
                        P.op("y", lambda e, dhbm=dhbm, dcol=dcol, ncol=ncol,
                             stx=stx: e.dma_start(
                                 out=bass.AP(dhbm.tensor, dcol,
                                             [[ncol, 128], [128 * ncol, 8],
                                              [1, 512]]),
                                 in_=bass.AP(stx, 0,
                                             [[4096, 128], [512, 8], [1, 512]])),
                             reads=[stxn], writes=[dname], kind="d")
                    for i in range(4):
                        sv, svn = stV[i % 2], f"stV{i % 2}"
                        for half in range(2):
                            for fc in range(8):
                                P.op("t", lambda e, i=i, half=half, fc=fc,
                                     xf=xf: e.matmul(
                                         p_vB[:, half * 512:(half + 1) * 512],
                                         xf[:, fc * 512 + i * 128:
                                            fc * 512 + (i + 1) * 128],
                                         wv_sb[:, fc * D + half * 512:
                                               fc * D + (half + 1) * 512],
                                         start=(fc == 0), stop=False),
                                     reads=[xfn, "wv_sb"], writes=["p_vB"])
                            P.op("t", lambda e, half=half: e.matmul(
                                p_vB[:, half * 512:(half + 1) * 512],
                                ones_row[:, 0:128],
                                bvrow[:, half * 512:(half + 1) * 512],
                                start=False, stop=True),
                                reads=["ones_row", "bvrow"], writes=["p_vB"])
                        P.op("v", lambda e, sv=sv: e.tensor_copy(sv[:], p_vB[:]),
                             reads=["p_vB"], writes=[svn])
                        P.op("y", lambda e, m=m, i=i, sv=sv: e.dma_start(
                            out=bass.AP(v_hbm.tensor, (m * 512 + i * 128) * DH,
                                        [[DH, 128], [NS * DH, H], [1, DH]]),
                            in_=bass.AP(sv, 0, [[D, 128], [DH, H], [1, DH]])),
                            reads=[svn], writes=["@v_hbm"], kind="d")

            P.barrier()


        NKP_NB = NKSNB // 2
        with ExitStack() as pc:
            k_sb = sbuf(pc, "k_sb", [128, NKP_NB * 128])
            k_fb = sbuf(pc, "k_fb", [67, C.FB])
            v_sb = sbuf(pc, "v_sb", [128, VSLOT * 65])
            q_sb = sbuf(pc, "q_sb", [128, T])
            q4_sb = sbuf(pc, "q4_sb", [67, QBLK])
            pkx_sb = sbuf(pc, "pkx_sb", [128, WU])
            pqx_sb = sbuf(pc, "pqx_sb", [128, WU])
            sat_q = sbuf(pc, "sat_q", [128, 2])
            sat_k = sbuf(pc, "sat_k", [128, 2])
            sq_scr = sbuf(pc, "sq_scr", [128, 512])
            nrm = sbuf(pc, "nrm", [1, 96])
            red = sbuf(pc, "red", [1, 16])
            negB = sbuf(pc, "negB", [128, 1])
            bias_nb = sbuf(pc, "bias_nb", [128, 2 * NKSNB])
            st_bq = [sbuf(pc, f"st_bq{i}", [128, 512]) for i in range(2)]
            st_bk = [sbuf(pc, f"st_bk{i}", [128, 512]) for i in range(2)]
            c2p_st = [sbuf(pc, f"c2p{i}", [128, 16 * 128]) for i in range(4)]
            p2c_st = [sbuf(pc, f"p2c{i}", [128, 512]) for i in range(2)]
            bias_sb = [sbuf(pc, f"bsb{i}", [128, 512]) for i in range(2)]
            pt_sb = [sbuf(pc, f"pt{i}", [128, 512]) for i in range(4)]
            w_sb = sbuf(pc, "w_sb", [1, 2 * QBLK])
            wbc = [sbuf(pc, f"wbc{i}", [65, 512]) for i in range(2)]
            ev1 = sbuf(pc, "ev1", [65, 512])
            ev2 = sbuf(pc, "ev2", [65, 512])
            rden = sbuf(pc, "rden", [1, 512])
            zrow = sbuf(pc, "zrow", [1, 512])
            fmask = sbuf(pc, "fmask", [2, C.FB])
            krowt = [sbuf(pc, f"krowt{i}", [1, 512]) for i in range(3)]
            qrowt = sbuf(pc, "qrowt", [1, 1024])
            nhcell = sbuf(pc, "nhcell", [1, 1])
            nhcell_i = pc.enter_context(
                nc.sbuf_tensor("nhcell_i", [1, 1], mybir.dt.int32))
            p_sc = [psum(pc, f"p_sc{i}", [128, 512]) for i in range(3)]
            p_ctx = psum(pc, "p_ctx", [65, 1536])
            p_g = [psum(pc, "p_g0", [128, 512])]
            p_bc = psum(pc, "p_bc", [128, 512])

            def pgb(i):
                return (p_g[0], "p_g0") if i % 2 == 0 else (p_bc, "p_bc")

            P.op("v", lambda e: e.memset(zrow[:], 0.0), writes=["zrow"])

            P.op("v", lambda e: e.store(nhcell_i[0:1, 0:1],
                                        _rv(regs["v"]["nH"], 0, 64) * 128),
                 writes=["nhcell_i"], dyn=True)
            P.op("v", lambda e: e.tensor_copy(nhcell[0:1, 0:1],
                                              nhcell_i[0:1, 0:1]),
                 reads=["nhcell_i"], writes=["nhcell"])
            P.op("g", lambda e: e.iota(
                fmask[:], [[1, C.FB]], base=0, channel_multiplier=0,
                allow_small_or_imprecise_dtypes=True),
                reads=["fmask"], writes=["fmask"], kind="g")
            P.op("t", lambda e: e.matmul(p_bc[0:2, 0:1], ones_row[0:1, 0:2],
                                         nhcell[0:1, 0:1],
                                         start=True, stop=True),
                 reads=["ones_row", "nhcell"], writes=["p_bc"])
            P.op("v", lambda e: e.tensor_copy(krowt[2][0:1, 0:1],
                                              p_bc[0:1, 0:1]),
                 reads=["p_bc"], writes=["krowt2"])
            P.op("v", lambda e: e.tensor_scalar(
                fmask[0:1, :], fmask[0:1, :], krowt[2][0:1, 0:1], None,
                ALU.is_ge),
                reads=["fmask", "krowt2"], writes=["fmask"])
            for cch in range((C.FB + 511) // 512):
                c0, c1 = cch * 512, min(C.FB, (cch + 1) * 512)
                P.op("s", lambda e, c0=c0, c1=c1: e.dma_start(
                    out=k_fb[65:66, c0:c1], in_=fmask[0:1, c0:c1]),
                    reads=["fmask"], writes=["k_fb_m"], kind="d")
                P.op("v", lambda e, c0=c0, c1=c1: e.tensor_scalar(
                    krowt[0][0:1, 0:c1 - c0], fmask[0:1, c0:c1], -1.0, 1.0,
                    ALU.mult, ALU.add),
                    reads=["fmask"], writes=["krowt0"])
                P.op("s", lambda e, c0=c0, c1=c1: e.dma_start(
                    out=k_fb[64:65, c0:c1], in_=krowt[0][0:1, 0:c1 - c0]),
                    reads=["krowt0"], writes=["k_fb_m"], kind="d")

            for cch in range((QBLK + 511) // 512):
                c0, c1 = cch * 512, min(QBLK, (cch + 1) * 512)
                P.op("s", lambda e, c0=c0, c1=c1: e.dma_start(
                    out=q4_sb[66:67, c0:c1], in_=ones_row[0:1, 0:c1 - c0]),
                    reads=["ones_row"], writes=["q4_sb"], kind="d")

            def vsl(kk):
                return 8 + kk if kk < NKSNB else 16 + kk

            for h in range(H):
                KW = NKP_NB * 128
                for h2 in range(2):
                    P.op("y", lambda e, h=h, h2=h2, KW=KW: e.dma_start(
                        out=bass.AP(k_sb, 64 * h2 * KW,
                                    [[KW, 64], [128, NKP_NB], [1, 128]]),
                        in_=bass.AP(k_hbm.tensor, 64 * h * NS + 128 * h2,
                                    [[NS, 64], [256, NKP_NB], [1, 128]])),
                        reads=["@k_hbm"], writes=["k_sb"], kind="d")
                P.op("y", lambda e, h=h: e.dma_start(
                    out=k_fb[0:64, :],
                    in_=bass.AP(k_hbm.tensor, 64 * h * NS + NB,
                                [[NS, 64], [1, C.FB]])),
                    reads=["@k_hbm"], writes=["k_fb"], kind="d")
                P.op("y", lambda e, h=h: e.dma_start(
                    out=bass.AP(v_sb, 8 * 65,
                                [[VSLOT * 65, 128], [65, NKSNB], [1, DH]]),
                    in_=bass.AP(v_hbm.tensor, h * NS * DH,
                                [[DH, 128], [128 * DH, NKSNB], [1, DH]])),
                    reads=["@v_hbm"], writes=["v_sb"], kind="d")
                P.op("y", lambda e, h=h: e.dma_start(
                    out=bass.AP(v_sb, (16 + NKSNB) * 65,
                                [[VSLOT * 65, 128], [65, FB128], [1, DH]]),
                    in_=bass.AP(v_hbm.tensor, (h * NS + NB) * DH,
                                [[DH, 128], [128 * DH, FB128], [1, DH]])),
                    reads=["@v_hbm"], writes=["v_sb"], kind="d")
                for hh in range(2):
                    P.op("y", lambda e, h=h, hh=hh: e.dma_start(
                        out=pkx_sb[64 * hh:64 * hh + 64, :],
                        in_=bass.AP(pkx_hbm.tensor, 64 * h * WU,
                                    [[WU, 64], [1, WU]])),
                        reads=["@pkx"], writes=["pkx_sb"], kind="d")
                    P.op("y", lambda e, h=h, hh=hh: e.dma_start(
                        out=pqx_sb[64 * hh:64 * hh + 64, :],
                        in_=bass.AP(pqx_hbm.tensor, 64 * h * WU,
                                    [[WU, 64], [1, WU]])),
                        reads=["@pqx"], writes=["pqx_sb"], kind="d")
                    P.op("y", lambda e, h=h, hh=hh: e.dma_start(
                        out=q_sb[64 * hh:64 * hh + 64, :],
                        in_=bass.AP(q_hbm.tensor, 64 * h * T,
                                    [[T, 64], [1, T]])),
                        reads=["@q_hbm"], writes=["q_sb"], kind="d")

                P.op("v", lambda e: e.memset(bass.AP(
                    v_sb, 8 * 65 + 64, [[VSLOT * 65, 128], [65, NKSNB], [1, 1]]),
                    1.0), reads=["v_sb"], writes=["v_sb"])
                P.op("v", lambda e: e.memset(bass.AP(
                    v_sb, (16 + NKSNB) * 65 + 64,
                    [[VSLOT * 65, 128], [65, FB128], [1, 1]]),
                    1.0), reads=["v_sb"], writes=["v_sb"])
                P.op("v", lambda e: e.memset(
                    v_sb[:, ds(RV_ZL, 8 * 65)], 0.0),
                    reads=["v_sb"], writes=["v_sb"], dyn=True)
                P.op("v", lambda e: e.memset(
                    v_sb[:, ds(RV_ZR, 8 * 65)], 0.0),
                    reads=["v_sb"], writes=["v_sb"], dyn=True)
                P.op("v", lambda e: e.memset(
                    v_sb[:, ds(RV_ZT, 8 * 65)], 0.0),
                    reads=["v_sb"], writes=["v_sb"], dyn=True)

                P.op("v", lambda e: e.tensor_copy(sat_q[:, 0:1],
                                                  pqx_sb[:, WU - 1:WU]),
                     reads=["pqx_sb"], writes=["sat_q"])
                P.op("v", lambda e: e.tensor_copy(sat_q[:, 1:2],
                                                  pqx_sb[:, 0:1]),
                     reads=["pqx_sb"], writes=["sat_q"])
                P.op("v", lambda e: e.tensor_copy(sat_k[0:64, 0:1],
                                                  pkx_sb[0:64, 0:1]),
                     reads=["pkx_sb"], writes=["sat_k"])
                P.op("v", lambda e: e.tensor_copy(sat_k[0:64, 1:2],
                                                  pkx_sb[0:64, WU - 1:WU]),
                     reads=["pkx_sb"], writes=["sat_k"])
                P.op("v", lambda e: e.tensor_copy(sat_k[64:128, 0:1],
                                                  pkx_sb[64:128, 0:1]),
                     reads=["pkx_sb"], writes=["sat_k"])
                P.op("v", lambda e: e.tensor_copy(sat_k[64:128, 1:2],
                                                  pkx_sb[64:128, WU - 1:WU]),
                     reads=["pkx_sb"], writes=["sat_k"])

                def norm_rows(src, srcname, plo, phi, n, dstoff):
                    nch = (n + 511) // 512
                    for cch in range(nch):
                        c0, c1 = cch * 512, min(n, (cch + 1) * 512)
                        pg, pgn = pgb(cch)
                        P.op("s", lambda e, c0=c0, c1=c1: e.square(
                            sq_scr[plo:phi, 0:c1 - c0], src[plo:phi, c0:c1]),
                            reads=[srcname], writes=["sq_scr"])
                        P.op("t", lambda e, c0=c0, c1=c1, pg=pg: e.matmul(
                            pg[0:1, 0:c1 - c0], ones_col[plo:phi, 0:1],
                            sq_scr[plo:phi, 0:c1 - c0], start=True,
                            stop=True),
                            reads=["sq_scr", "ones_col"], writes=[pgn])
                        P.op("v", lambda e, c0=c0, c1=c1, pg=pg, cch=cch:
                             e.reduce_max(
                                 nrm[0:1, dstoff + cch:dstoff + cch + 1],
                                 pg[0:1, 0:c1 - c0],
                                 axis=mybir.AxisListType.X),
                             reads=[pgn], writes=["nrm"])
                    return nch

                n1 = norm_rows(k_sb, "k_sb", 0, 64, NKP_NB * 128, 0)
                n2 = norm_rows(k_sb, "k_sb", 64, 128, NKP_NB * 128, n1)
                n3 = norm_rows(k_fb, "k_fb", 0, 64, C.FB, n1 + n2)
                P.op("v", lambda e, n=n1 + n2 + n3: e.reduce_max(
                    red[0:1, 0:1], nrm[0:1, 0:n],
                    axis=mybir.AxisListType.X), reads=["nrm"], writes=["red"])
                nq = norm_rows(q_sb, "q_sb", 0, 64, T, 0)
                P.op("v", lambda e, nq=nq: e.reduce_max(
                    red[0:1, 1:2], nrm[0:1, 0:nq], axis=mybir.AxisListType.X),
                    reads=["nrm"], writes=["red"])
                npk = norm_rows(pkx_sb, "pkx_sb", 0, 64, WU, 0)
                P.op("v", lambda e, npk=npk: e.reduce_max(
                    red[0:1, 2:3], nrm[0:1, 0:npk], axis=mybir.AxisListType.X),
                    reads=["nrm"], writes=["red"])
                npq = norm_rows(pqx_sb, "pqx_sb", 0, 64, WU, 0)
                P.op("v", lambda e, npq=npq: e.reduce_max(
                    red[0:1, 3:4], nrm[0:1, 0:npq], axis=mybir.AxisListType.X),
                    reads=["nrm"], writes=["red"])
                P.op("v", lambda e: e.tensor_mul(red[0:1, 4:5], red[0:1, 0:1],
                                                 red[0:1, 1:2]),
                     reads=["red"], writes=["red"])
                P.op("v", lambda e: e.tensor_mul(red[0:1, 5:6], red[0:1, 1:2],
                                                 red[0:1, 2:3]),
                     reads=["red"], writes=["red"])
                P.op("v", lambda e: e.tensor_mul(red[0:1, 6:7], red[0:1, 0:1],
                                                 red[0:1, 3:4]),
                     reads=["red"], writes=["red"])
                P.op("s", lambda e: e.activation(red[0:1, 7:10], red[0:1, 4:7],
                                                 AF.Ln, bias=cst[0:1, 0:1],
                                                 scale=1.0),
                     reads=["red", "cst"], writes=["red"])
                P.op("s", lambda e: e.activation(red[0:1, 7:10], red[0:1, 7:10],
                                                 AF.Exp, bias=0.0, scale=0.5),
                     reads=["red"], writes=["red"])
                P.op("v", lambda e: e.tensor_add(red[0:1, 10:11], red[0:1, 7:8],
                                                 red[0:1, 8:9]),
                     reads=["red"], writes=["red"])
                P.op("v", lambda e: e.tensor_add(red[0:1, 10:11],
                                                 red[0:1, 10:11],
                                                 red[0:1, 9:10]),
                     reads=["red"], writes=["red"])
                P.op("v", lambda e: e.tensor_scalar(
                    red[0:1, 11:12], red[0:1, 10:11],
                    -SCALE_INV * 1.02, -1.0, ALU.mult, ALU.add),
                    reads=["red"], writes=["red"])
                P.op("t", lambda e: e.matmul(p_bc[:, 0:1],
                                             ones_row[0:1, 0:128],
                                             red[0:1, 11:12],
                                             start=True, stop=True),
                     reads=["ones_row", "red"], writes=["p_bc"])
                P.op("v", lambda e: e.tensor_copy(negB[:], p_bc[:, 0:1]),
                     reads=["p_bc"], writes=["negB"])

                for kk in range(NKSNB):
                    half, pairc = kk % 2, (kk // 2) * 128
                    pg, pgn = pgb(kk)
                    P.op("t", lambda e, half=half, pairc=pairc, pg=pg: e.matmul(
                        pg[:, 0:2],
                        k_sb[64 * half:64 * half + 64, pairc:pairc + 128],
                        sat_q[64 * half:64 * half + 64, 0:2],
                        start=True, stop=True),
                        reads=["k_sb", "sat_q"], writes=[pgn])
                    P.op("v", lambda e, kk=kk, pg=pg: e.tensor_scalar(
                        bias_nb[:, 2 * kk:2 * kk + 2], pg[:, 0:2], SCALE_INV,
                        negB[:, 0:1], ALU.mult, ALU.add),
                        reads=[pgn, "negB"], writes=["bias_nb"])

                nfch = (C.FB + 511) // 512
                for cch in range(nfch):
                    c0, c1 = cch * 512, min(C.FB, (cch + 1) * 512)
                    for side in range(2):
                        pg, pgn = pgb(side)
                        P.op("t", lambda e, side=side, c0=c0, c1=c1, pg=pg:
                             e.matmul(
                                 pg[0:1, 0:c1 - c0],
                                 sat_q[0:64, side:side + 1],
                                 k_fb[0:64, c0:c1], start=True, stop=True),
                             reads=["sat_q", "k_fb", "k_fb_m"], writes=[pgn])
                        P.op("v", lambda e, side=side, c0=c0, c1=c1, pg=pg:
                             e.tensor_copy(krowt[side][0:1, 0:c1 - c0],
                                           pg[0:1, 0:c1 - c0]),
                             reads=[pgn], writes=[f"krowt{side}"])
                    P.op("v", lambda e, c0=c0, c1=c1: e.tensor_sub(
                        krowt[1][0:1, 0:c1 - c0], krowt[1][0:1, 0:c1 - c0],
                        krowt[0][0:1, 0:c1 - c0]),
                        reads=["krowt0", "krowt1"], writes=["krowt1"])
                    P.op("v", lambda e, c0=c0, c1=c1: e.tensor_mul(
                        krowt[1][0:1, 0:c1 - c0], krowt[1][0:1, 0:c1 - c0],
                        fmask[0:1, c0:c1]),
                        reads=["krowt1", "fmask"], writes=["krowt1"])
                    P.op("v", lambda e, c0=c0, c1=c1: e.tensor_add(
                        krowt[0][0:1, 0:c1 - c0], krowt[0][0:1, 0:c1 - c0],
                        krowt[1][0:1, 0:c1 - c0]),
                        reads=["krowt0", "krowt1"], writes=["krowt0"])
                    P.op("s", lambda e, c0=c0, c1=c1: e.dma_start(
                        out=k_fb[66:67, c0:c1], in_=krowt[0][0:1, 0:c1 - c0]),
                        reads=["krowt0"], writes=["k_fb_m"], kind="d")

                for qb in range(NQB):
                    q0b = qb * QBLK
                    for uc in range((QBLK + 511) // 512):
                        c0, c1 = uc * 512, min(QBLK, (uc + 1) * 512)
                        P.op("v", lambda e, c0=c0, c1=c1, q0b=q0b:
                             e.tensor_copy(q4_sb[0:64, c0:c1],
                                           q_sb[0:64, q0b + c0:q0b + c1]),
                             reads=["q_sb"], writes=["q4_sb"])
                        for side in range(2):
                            pg, pgn = pgb(side)
                            P.op("t", lambda e, c0=c0, c1=c1, q0b=q0b, pg=pg,
                                 side=side: e.matmul(
                                     pg[0:1, 0:c1 - c0],
                                     sat_k[64 * side:64 * side + 64,
                                           side:side + 1],
                                     q_sb[64 * side:64 * side + 64,
                                          q0b + c0:q0b + c1],
                                     start=True, stop=True),
                                 reads=["sat_k", "q_sb"], writes=[pgn])
                            P.op("v", lambda e, c0=c0, c1=c1, pg=pg,
                                 side=side: e.tensor_copy(
                                     qrowt[0:1,
                                           side * 512:side * 512 + c1 - c0],
                                     pg[0:1, 0:c1 - c0]),
                                 reads=[pgn], writes=["qrowt"])
                            P.op("s", lambda e, c0=c0, c1=c1, side=side:
                                 e.dma_start(
                                     out=q4_sb[64 + side:65 + side, c0:c1],
                                     in_=qrowt[0:1, side * 512:
                                               side * 512 + c1 - c0]),
                                 reads=["qrowt"], writes=["q4_sb"], kind="d")
                            P.op("s", lambda e, c0=c0, c1=c1, pg=pg, side=side:
                                 e.activation(
                                     w_sb[0:1, side * QBLK + c0:
                                          side * QBLK + c1],
                                     pg[0:1, 0:c1 - c0], AF.Exp,
                                     bias=0.0, scale=SCALE_INV),
                                 reads=[pgn], writes=["w_sb"])
                    for ac in range(QBLK // 128):
                        for uc in range(NUC):
                            u0, u1 = uc * 512, min(WU, (uc + 1) * 512)
                            hb = 64 * (uc % 2)
                            pg, pgn = pgb(uc)
                            stq, stqn = st_bq[uc % 2], f"st_bq{uc % 2}"
                            P.op("t", lambda e, ac=ac, u0=u0, u1=u1, q0b=q0b,
                                 pg=pg, hb=hb: e.matmul(
                                     pg[:, 0:u1 - u0],
                                     q_sb[hb:hb + 64,
                                          q0b + ac * 128:q0b + (ac + 1) * 128],
                                     pkx_sb[hb:hb + 64, u0:u1],
                                     start=True, stop=True),
                                 reads=["q_sb", "pkx_sb"], writes=[pgn])
                            P.op("v", lambda e, u0=u0, u1=u1, pg=pg, stq=stq:
                                 e.tensor_copy(stq[:, 0:u1 - u0],
                                               pg[:, 0:u1 - u0]),
                                 reads=[pgn], writes=[stqn])
                            P.op("y", lambda e, ac=ac, u0=u0, u1=u1, stq=stq:
                                 e.dma_start(
                                     out=bass.AP(bqpr.tensor,
                                                 ac * 128 * WU + u0,
                                                 [[WU, 128], [1, u1 - u0]]),
                                     in_=stq[:, 0:u1 - u0]),
                                 reads=[stqn], writes=["@bqpr"], kind="d")
                    if qb == 0:
                        for uc in range(NUC):
                            u0, u1 = uc * 512, min(WU, (uc + 1) * 512)
                            for r in range(C.bk_lo, C.bk_hi + 1):
                                half, pairc = r % 2, (r // 2) * 128
                                pg, pgn = pgb(r)
                                stk, stkn = st_bk[r % 2], f"st_bk{r % 2}"
                                P.op("t", lambda e, half=half, pairc=pairc,
                                     u0=u0, u1=u1, pg=pg: e.matmul(
                                         pg[:, 0:u1 - u0],
                                         k_sb[64 * half:64 * half + 64,
                                              pairc:pairc + 128],
                                         pqx_sb[64 * half:64 * half + 64,
                                                u0:u1],
                                         start=True, stop=True),
                                     reads=["k_sb", "pqx_sb"], writes=[pgn])
                                P.op("v", lambda e, u0=u0, u1=u1, pg=pg,
                                     stk=stk: e.tensor_copy(
                                         stk[:, 0:u1 - u0], pg[:, 0:u1 - u0]),
                                     reads=[pgn], writes=[stkn])
                                P.op("y", lambda e, r=r, u0=u0, u1=u1, stk=stk:
                                     e.dma_start(
                                         out=bass.AP(
                                             bkpr.tensor,
                                             (r - C.bk_lo) * 128 * WU + u0,
                                             [[WU, 128], [1, u1 - u0]]),
                                         in_=stk[:, 0:u1 - u0]),
                                     reads=[stkn], writes=["@bkpr"],
                                     kind="d")

                    for qc in range(NQC):
                        qcg = q0b + qc * QC
                        a_list = [qcg // 128 + i for i in range(QC // 128)]
                        for side in range(2):
                            P.op("t", lambda e, side=side, qc=qc, q0b=q0b:
                                 e.matmul(
                                     p_bc[0:65, 0:QC], ones_row[0:1, 0:65],
                                     w_sb[0:1, side * QBLK + qc * QC:
                                          side * QBLK + qc * QC + QC],
                                     start=True, stop=True),
                                 reads=["ones_row", "w_sb"], writes=["p_bc"])
                            P.op("v", lambda e, side=side: e.tensor_copy(
                                wbc[side][:, 0:QC], p_bc[0:65, 0:QC]),
                                reads=["p_bc"], writes=[f"wbc{side}"])
                        for side in range(3):
                            P.op("t", lambda e, side=side: e.matmul(
                                p_ctx[:, 512 * side:512 * side + QC],
                                zrow[0:1, 0:65], zrow[0:1, 0:QC],
                                start=True, stop=False),
                                reads=["zrow"], writes=["p_ctx"])
                        for ia, a in enumerate(a_list):
                            lo, hi = C.nearset[a]
                            nr = hi - lo + 1
                            aloc = a - q0b // 128
                            u0 = C.UPAD - 128 * (a - lo) - NPAD - 127
                            assert 0 <= u0 and u0 + 128 * (nr - 1) + 254 < WU
                            cs, csn = c2p_st[ia], f"c2p{ia}"
                            P.op("s", lambda e, aloc=aloc, u0=u0, nr=nr, cs=cs:
                                 e.dma_start(
                                     out=cs[:, 0:nr * 128],
                                     in_=bass.AP(bqpr.tensor,
                                                 aloc * 128 * WU + u0 + 127,
                                                 [[WU - 1, 128], [128, nr],
                                                  [1, 128]])),
                                 reads=["@bqpr"], writes=[csn], kind="d")
                        near_rs = {}
                        for ia, a in enumerate(a_list):
                            lo, hi = C.nearset[a]
                            for r in range(lo, hi + 1):
                                near_rs.setdefault(r, []).append(ia)
                        for r, ias in near_rs.items():
                            assert ias == list(range(ias[0], ias[0] + len(ias)))

                        def segs_of(kk):
                            segs = []
                            for ia, a in enumerate(a_list):
                                c = C.cls(a, kk)
                                if segs and segs[-1][2] == c:
                                    segs[-1][1] += 128
                                else:
                                    segs.append([ia * 128, 128, c])
                            return segs

                        def emit_qk(kk, qcg=qcg, qc=qc, a_list=a_list,
                                    near_rs=near_rs, segs_of=segs_of,
                                    part="all"):
                            psc, pscn = p_sc[kk % 3], f"p_sc{kk % 3}"
                            ptb, ptbn = pt_sb[kk % 4], f"pt{kk % 4}"
                            if kk < NKSNB:
                                half, pairc = kk % 2, (kk // 2) * 128
                                if part != "rest":
                                    P.op("t", lambda e, half=half, pairc=pairc,
                                         qcg=qcg, psc=psc: e.matmul(
                                             psc[:, 0:QC],
                                             k_sb[64 * half:64 * half + 64,
                                                  pairc:pairc + 128],
                                             q_sb[64 * half:64 * half + 64,
                                                  qcg:qcg + QC],
                                             start=True, stop=True),
                                         reads=["k_sb", "q_sb"],
                                         writes=[pscn])
                                if part == "mm":
                                    return
                                segs = segs_of(kk)
                                nsegs = [s0 for s0 in segs if s0[2] == "N"]
                                if nsegs:
                                    (noff, nlen, _) = nsegs[0]
                                    ias = near_rs[kk]
                                    na = len(ias)
                                    a0 = a_list[ias[0]]
                                    u0p = 128 * (a0 - kk) + NPAD + C.UPAD - 127
                                    assert 0 <= u0p and \
                                        u0p + 128 * (na - 1) + 254 < WU
                                    pstj, pstjn = p2c_st[kk % 2], f"p2c{kk % 2}"
                                    P.op("s", lambda e, kk=kk, u0p=u0p, na=na,
                                         pstj=pstj: e.dma_start(
                                             out=pstj[:, 0:na * 128],
                                             in_=bass.AP(
                                                 bkpr.tensor,
                                                 (kk - C.bk_lo) * 128 * WU
                                                 + u0p + 127,
                                                 [[WU - 1, 128], [128, na],
                                                  [1, 128]])),
                                         reads=["@bkpr"], writes=[pstjn],
                                         kind="d")
                                    bsb, bsbn = bias_sb[kk % 2], f"bsb{kk % 2}"
                                    ptr, ptrn = pgb(kk)
                                    for j, ia in enumerate(ias):
                                        a = a_list[ia]
                                        lo, hi = C.nearset[a]
                                        cs, csn = c2p_st[ia], f"c2p{ia}"
                                        P.op("t", lambda e, j=j, kk=kk, lo=lo,
                                             cs=cs, ptr=ptr: e.transpose(
                                                 ptr[:, j * 128:(j + 1) * 128],
                                                 cs[:, (kk - lo) * 128:
                                                    (kk - lo + 1) * 128],
                                                 ident[:]),
                                             reads=[csn, "ident"],
                                             writes=[ptrn])
                                    P.op("v", lambda e, nlen=nlen, bsb=bsb,
                                         ptr=ptr, pstj=pstj: e.tensor_add(
                                             bsb[:, 0:nlen], ptr[:, 0:nlen],
                                             pstj[:, 0:nlen]),
                                         reads=[ptrn, pstjn], writes=[bsbn])
                                    P.op("v", lambda e, noff=noff, nlen=nlen,
                                         psc=psc, bsb=bsb: e.tensor_add(
                                             psc[:, noff:noff + nlen],
                                             psc[:, noff:noff + nlen],
                                             bsb[:, 0:nlen]),
                                         reads=[pscn, bsbn], writes=[pscn])
                                for (soff, slen, c) in segs:
                                    if c == "N":
                                        bcol = lambda e: negB[:, 0:1]
                                    elif c == "H":
                                        bcol = lambda e, kk=kk: bias_nb[
                                            :, 2 * kk:2 * kk + 1]
                                    else:
                                        bcol = lambda e, kk=kk: bias_nb[
                                            :, 2 * kk + 1:2 * kk + 2]
                                    P.op("s", lambda e, soff=soff, slen=slen,
                                         bcol=bcol, psc=psc, ptb=ptb:
                                         e.activation(
                                             ptb[:, soff:soff + slen],
                                             psc[:, soff:soff + slen], AF.Exp,
                                             bias=bcol(e), scale=SCALE_INV),
                                         reads=[pscn, "bias_nb", "negB"],
                                         writes=[ptbn])
                            else:
                                f = kk - NKSNB
                                P.op("t", lambda e, f=f, qc=qc, psc=psc:
                                     e.matmul(
                                         psc[:, 0:QC],
                                         k_fb[:, f * 128:(f + 1) * 128],
                                         q4_sb[:, qc * QC:qc * QC + QC],
                                         start=True, stop=True),
                                     reads=["k_fb", "k_fb_m", "q4_sb"],
                                     writes=[pscn])
                                P.op("s", lambda e, psc=psc, ptb=ptb:
                                     e.activation(
                                         ptb[:, 0:QC], psc[:, 0:QC], AF.Exp,
                                         bias=negB[:, 0:1], scale=SCALE_INV),
                                     reads=[pscn, "negB"], writes=[ptbn])

                        def emit_pv(kk, segs_of=segs_of):
                            ptb, ptbn = pt_sb[kk % 4], f"pt{kk % 4}"
                            slot = vsl(kk)
                            if kk < NKSNB:
                                for (soff, slen, c) in segs_of(kk):
                                    base = {"N": 0, "H": 512, "L": 1024}[c]
                                    P.op("t", lambda e, slot=slot, soff=soff,
                                         slen=slen, base=base, ptb=ptb:
                                         e.matmul(
                                             p_ctx[:, base + soff:
                                                   base + soff + slen],
                                             v_sb[:, slot * 65:slot * 65 + 65],
                                             ptb[:, soff:soff + slen],
                                             start=False, stop=False),
                                         reads=["v_sb", ptbn, "p_ctx"],
                                         writes=["p_ctx"])
                            else:
                                P.op("t", lambda e, slot=slot, ptb=ptb:
                                     e.matmul(
                                         p_ctx[:, 0:QC],
                                         v_sb[:, slot * 65:slot * 65 + 65],
                                         ptb[:, 0:QC],
                                         start=False, stop=False),
                                     reads=["v_sb", ptbn, "p_ctx"],
                                     writes=["p_ctx"])

                        pv_next = [0]

                        def pv_upto(k):
                            while pv_next[0] <= k:
                                emit_pv(pv_next[0])
                                pv_next[0] += 1

                        for m in range(0, NKSNB, 2):
                            emit_qk(m, part="mm")
                            emit_qk(m + 1, part="mm")
                            emit_qk(m, part="rest")
                            emit_qk(m + 1, part="rest")
                            pv_upto(m - 1)
                        for kk in range(NKSNB, NKSNB + FB128):
                            emit_qk(kk)
                            pv_upto(kk - 2)
                        pv_upto(NKSNB + FB128 - 1)

                        for side in range(3):
                            P.op("t", lambda e, side=side: e.matmul(
                                p_ctx[:, 512 * side:512 * side + QC],
                                zrow[0:1, 0:65], zrow[0:1, 0:QC],
                                start=False, stop=True),
                                reads=["zrow", "p_ctx"], writes=["p_ctx"])

                        P.op("v", lambda e: e.tensor_mul(
                            ev1[:, 0:QC], wbc[0][:, 0:QC],
                            p_ctx[:, 512:512 + QC]),
                            reads=["wbc0", "p_ctx"], writes=["ev1"])
                        P.op("v", lambda e: e.tensor_mul(
                            ev2[:, 0:QC], wbc[1][:, 0:QC],
                            p_ctx[:, 1024:1024 + QC]),
                            reads=["wbc1", "p_ctx"], writes=["ev2"])
                        P.op("v", lambda e: e.tensor_add(
                            ev1[:, 0:QC], ev1[:, 0:QC], ev2[:, 0:QC]),
                            reads=["ev1", "ev2"], writes=["ev1"])
                        P.op("v", lambda e: e.tensor_add(
                            ev1[:, 0:QC], ev1[:, 0:QC], p_ctx[:, 0:QC]),
                            reads=["ev1", "p_ctx"], writes=["ev1"])
                        P.op("v", lambda e: e.reciprocal(
                            rden[0:1, 0:QC], ev1[64:65, 0:QC]),
                            reads=["ev1"], writes=["rden"])
                        P.op("t", lambda e: e.matmul(
                            p_bc[0:64, 0:QC], ones_row[0:1, 0:64],
                            rden[0:1, 0:QC], start=True, stop=True),
                            reads=["ones_row", "rden"], writes=["p_bc"])
                        P.op("v", lambda e: e.tensor_mul(
                            ev2[0:64, 0:QC], ev1[0:64, 0:QC],
                            p_bc[0:64, 0:QC]),
                            reads=["ev1", "p_bc"], writes=["ev2"])
                        P.op("y", lambda e, h=h, qcg=qcg: e.dma_start(
                            out=bass.AP(attn_hbm.tensor, 64 * h * T + qcg,
                                        [[T, 64], [1, QC]]),
                            in_=ev2[0:64, 0:QC]),
                            reads=["ev2"], writes=["@attn"], kind="d")

            P.barrier()

        with ExitStack() as pd:
            x1 = sbuf(pd, "x1", [128, 8 * T])
            y_acc = sbuf(pd, "y_acc", [128, T128 * D])
            xtd = [sbuf(pd, "xtD0", [128, D])]
            at_st = [sbuf(pd, "at_st0", [128, 512])]
            mrow = sbuf(pd, "mrow", [1, T])
            rrow = sbuf(pd, "rrow", [1, T])
            tr1 = sbuf(pd, "tr1", [1, 512])
            tmp_t = sbuf(pd, "tmp_t", [128, 512])
            w1g = [sbuf(pd, f"w1g{i}", [128, 8 * 256]) for i in range(2)]
            w2g = [sbuf(pd, f"w2g{i}", [128, 2 * D]) for i in range(2)]
            h_sb = sbuf(pd, "h_sb", [128, 6 * 512])
            b1c = sbuf(pd, "b1c", [128, E * (HF // 128)])
            b2r = sbuf(pd, "b2r", [E, D])
            wr_sb = sbuf(pd, "wr_sb", [128, 8 * E])
            probs = sbuf(pd, "probs", [128, 8])
            tmp8 = sbuf(pd, "tmp8", [128, 8])
            r1c = sbuf(pd, "r1c", [128, 1])
            r2c = sbuf(pd, "r2c", [128, 1])
            comb = sbuf(pd, "comb", [128, T128 * E])
            mcol = sbuf(pd, "mcol", [128, 1])
            rcol = sbuf(pd, "rcol", [128, 1])
            p_trD = psum(pd, "p_trD", [128, 512])
            p_mD = [psum(pd, f"p_mD{i}", [128, 512]) for i in range(2)]
            p_y = [psum(pd, f"p_y{i}", [128, D]) for i in range(2)]
            p_r = psum(pd, "p_r", [128, 512])

            for tc in range(T // 512):
                m = NPAD // 512 + tc
                for i in range(4):
                    xti, xtin = xtd[0], "xtD0"
                    P.op("s", lambda e, m=m, i=i, xti=xti: e.dma_start(
                        out=xti[:],
                        in_=xin[m * 512 + i * 128:m * 512 + (i + 1) * 128, :]),
                        reads=["@xin"], writes=[xtin], kind="d")
                    for fc in range(8):
                        P.op("t", lambda e, fc=fc, xti=xti, i=i: e.transpose(
                            p_trD[:, 0:128],
                            xti[:, fc * 128:(fc + 1) * 128], ident[:]),
                            reads=[xtin, "ident"], writes=["p_trD"])
                        P.op("v", lambda e, fc=fc, tc=tc, i=i: e.tensor_copy(
                            x1[:, fc * T + tc * 512 + i * 128:
                               fc * T + tc * 512 + (i + 1) * 128],
                            p_trD[:, 0:128]),
                            reads=["p_trD"], writes=["x1"])
            for fc in range(8):
                for tc in range(T // 512):
                    ai, ain = at_st[0], "at_st0"
                    P.op("y", lambda e, fc=fc, tc=tc, ai=ai: e.dma_start(
                        out=ai[:],
                        in_=bass.AP(attn_hbm.tensor, fc * 128 * T + tc * 512,
                                    [[T, 128], [1, 512]])),
                        reads=["@attn"], writes=[ain], kind="d")
                    P.op("v", lambda e, fc=fc, tc=tc, ai=ai: e.tensor_add(
                        x1[:, fc * T + tc * 512:fc * T + (tc + 1) * 512],
                        x1[:, fc * T + tc * 512:fc * T + (tc + 1) * 512],
                        ai[:]),
                        reads=["x1", ain], writes=["x1"])

            def ln_fm():
                for tc in range(T // 512):
                    pm, pmn = p_mD[tc % 2], f"p_mD{tc % 2}"
                    for fc in range(8):
                        P.op("t", lambda e, fc=fc, tc=tc, pm=pm: e.matmul(
                            pm[0:1, :], ones_col[:, 0:1],
                            x1[:, fc * T + tc * 512:fc * T + (tc + 1) * 512],
                            start=(fc == 0), stop=(fc == 7)),
                            reads=["x1", "ones_col"], writes=[pmn])
                    P.op("v", lambda e, tc=tc, pm=pm: e.tensor_scalar(
                        mrow[0:1, tc * 512:(tc + 1) * 512], pm[0:1, :],
                        1.0 / D, None, ALU.mult),
                        reads=[pmn], writes=["mrow"])
                for tc in range(T // 512):
                    pm, pmn = p_mD[tc % 2], f"p_mD{tc % 2}"
                    for fc in range(8):
                        P.op("s", lambda e, fc=fc, tc=tc: e.square(
                            tmp_t[:, 0:512],
                            x1[:, fc * T + tc * 512:fc * T + (tc + 1) * 512]),
                            reads=["x1"], writes=["tmp_t"])
                        P.op("t", lambda e, fc=fc, pm=pm: e.matmul(
                            pm[0:1, :], ones_col[:, 0:1], tmp_t[:, 0:512],
                            start=(fc == 0), stop=(fc == 7)),
                            reads=["tmp_t", "ones_col"], writes=[pmn])
                    P.op("v", lambda e, tc=tc, pm=pm: e.tensor_scalar(
                        rrow[0:1, tc * 512:(tc + 1) * 512], pm[0:1, :],
                        1.0 / D, None, ALU.mult),
                        reads=[pmn], writes=["rrow"])
                    P.op("v", lambda e, tc=tc: e.tensor_mul(
                        tr1[0:1, :], mrow[0:1, tc * 512:(tc + 1) * 512],
                        mrow[0:1, tc * 512:(tc + 1) * 512]),
                        reads=["mrow"], writes=["tr1"])
                    P.op("v", lambda e, tc=tc: e.tensor_sub(
                        rrow[0:1, tc * 512:(tc + 1) * 512],
                        rrow[0:1, tc * 512:(tc + 1) * 512], tr1[0:1, :]),
                        reads=["rrow", "tr1"], writes=["rrow"])
                P.op("s", lambda e: e.activation(rrow[0:1, 0:T],
                                                 rrow[0:1, 0:T], AF.Ln,
                                                 bias=cst[0:1, 1:2], scale=1.0),
                     reads=["rrow", "cst"], writes=["rrow"])
                P.op("s", lambda e: e.activation(rrow[0:1, 0:T],
                                                 rrow[0:1, 0:T], AF.Exp,
                                                 bias=0.0, scale=-0.5),
                     reads=["rrow"], writes=["rrow"])
                for tc in range(T // 512):
                    P.op("t", lambda e, tc=tc: e.matmul(
                        p_mD[0][:, :], ones_row[0:1, 0:128],
                        mrow[0:1, tc * 512:(tc + 1) * 512],
                        start=True, stop=True),
                        reads=["ones_row", "mrow"], writes=["p_mD0"])
                    P.op("t", lambda e, tc=tc: e.matmul(
                        p_mD[1][:, :], ones_row[0:1, 0:128],
                        rrow[0:1, tc * 512:(tc + 1) * 512],
                        start=True, stop=True),
                        reads=["ones_row", "rrow"], writes=["p_mD1"])
                    for fc in range(8):
                        P.op("v", lambda e, fc=fc, tc=tc: e.tensor_sub(
                            tmp_t[:, 0:512],
                            x1[:, fc * T + tc * 512:fc * T + (tc + 1) * 512],
                            p_mD[0][:, :]),
                            reads=["x1", "p_mD0"], writes=["tmp_t"])
                        P.op("v", lambda e, fc=fc, tc=tc: e.tensor_mul(
                            x1[:, fc * T + tc * 512:fc * T + (tc + 1) * 512],
                            tmp_t[:, 0:512], p_mD[1][:, :]),
                            reads=["tmp_t", "p_mD1"], writes=["x1"])

            ln_fm()

            for fc in range(8):
                P.op("y", lambda e, fc=fc: e.dma_start(
                    out=bass.AP(wr_sb, fc * E, [[8 * E, 128], [1, E]]),
                    in_=bass.AP(wrt.tensor, fc * 128, [[1, 128], [D, E]]),
                    allow_slow_non_contiguous=True),
                    reads=["@wr"], writes=["wr_sb"], kind="d")
            P.op("y", lambda e: e.dma_start(out=b2r[:], in_=b2t),
                 reads=["@b2"], writes=["b2r"], kind="d")
            for js in range(HF // 128):
                P.op("y", lambda e, js=js: e.dma_start(
                    out=bass.AP(b1c, js, [[E * (HF // 128), 128],
                                          [HF // 128, E]]),
                    in_=bass.AP(b1t.tensor, js * 128, [[1, 128], [HF, E]]),
                    allow_slow_non_contiguous=True),
                    reads=["@b1"], writes=["b1c"], kind="d")

            for tcb in range(T128):
                tc, i = tcb // 4, tcb % 4
                pm, pmn = p_mD[tcb % 2], f"p_mD{tcb % 2}"
                for fc in range(8):
                    P.op("t", lambda e, tcb=tcb, fc=fc, pm=pm: e.matmul(
                        pm[:, 0:E],
                        x1[:, fc * T + tcb * 128:fc * T + (tcb + 1) * 128],
                        wr_sb[:, fc * E:(fc + 1) * E],
                        start=(fc == 0), stop=(fc == 7)),
                        reads=["x1", "wr_sb"], writes=[pmn])
                P.op("v", lambda e, pm=pm: e.reduce_max(
                    r1c[:], pm[:, 0:E], axis=mybir.AxisListType.X,
                    negate=True),
                    reads=[pmn], writes=["r1c"])
                P.op("s", lambda e, pm=pm: e.activation(
                    probs[:], pm[:, 0:E], AF.Exp, bias=r1c[:], scale=1.0),
                    reads=[pmn, "r1c"], writes=["probs"])
                P.op("v", lambda e: e.reduce_sum(
                    r2c[:], probs[:], axis=mybir.AxisListType.X),
                    reads=["probs"], writes=["r2c"])
                P.op("v", lambda e: e.reciprocal(r1c[:], r2c[:]),
                     reads=["r2c"], writes=["r1c"])
                P.op("v", lambda e: e.tensor_scalar(
                    probs[:], probs[:], r1c[:], None, ALU.mult),
                    reads=["probs", "r1c"], writes=["probs"])
                P.op("v", lambda e: e.max(out=tmp8[:], in_=probs[:]),
                     reads=["probs"], writes=["tmp8"])
                P.op("v", lambda e, tcb=tcb: e.tensor_scalar(
                    comb[:, tcb * E:(tcb + 1) * E], probs[:],
                    tmp8[:, 1:2], None, ALU.is_ge),
                    reads=["probs", "tmp8"], writes=["comb"])
                P.op("v", lambda e, tcb=tcb: e.tensor_mul(
                    comb[:, tcb * E:(tcb + 1) * E],
                    comb[:, tcb * E:(tcb + 1) * E], probs[:]),
                    reads=["probs", "comb"], writes=["comb"])
                P.op("t", lambda e, tcb=tcb: e.transpose(
                    p_r[0:8, 0:128], comb[:, tcb * E:(tcb + 1) * E],
                    ident[:]),
                    reads=["comb", "ident"], writes=["p_r"])
                P.op("v", lambda e, tcb=tcb: e.tensor_copy(
                    h_sb[0:8, tcb * 128:(tcb + 1) * 128], p_r[0:8, 0:128]),
                    reads=["p_r"], writes=[f"h{0 if tcb < 8 else 1}"])

            for tcb in range(T128):
                for fc in range(8):
                    P.op("t", lambda e, tcb=tcb, fc=fc: e.transpose(
                        p_trD[:, 0:128],
                        x1[:, fc * T + tcb * 128:fc * T + (tcb + 1) * 128],
                        ident[:]),
                        reads=["x1", "ident"], writes=["p_trD"])
                    P.op("v", lambda e, tcb=tcb, fc=fc: e.tensor_copy(
                        y_acc[:, tcb * D + fc * 128:tcb * D + (fc + 1) * 128],
                        p_trD[:, 0:128]),
                        reads=["p_trD"], writes=["y_acc"])
                for half in range(2):
                    P.op("t", lambda e, tcb=tcb, half=half: e.matmul(
                        p_trD[:, 0:512],
                        h_sb[0:8, tcb * 128:(tcb + 1) * 128],
                        b2r[:, half * 512:(half + 1) * 512],
                        start=True, stop=True),
                        reads=[f"h{0 if tcb < 8 else 1}", "b2r"],
                        writes=["p_trD"])
                    P.op("v", lambda e, tcb=tcb, half=half: e.tensor_add(
                        y_acc[:, tcb * D + half * 512:
                              tcb * D + (half + 1) * 512],
                        y_acc[:, tcb * D + half * 512:
                              tcb * D + (half + 1) * 512],
                        p_trD[:, 0:512]),
                        reads=["y_acc", "p_trD"], writes=["y_acc"])

            NJG = HF // 256
            NTC = T // 512
            NSL = E * NJG
            G = NSL * NTC

            def slab_load(sl):
                ee, jg = sl // NJG, sl % NJG
                b = sl % 2
                wg1, wg1n = w1g[b], f"w1g{b}"
                wg2, wg2n = w2g[b], f"w2g{b}"
                P.op("y", lambda e, ee=ee, jg=jg, wg1=wg1: e.dma_start(
                    out=wg1[:],
                    in_=bass.AP(w1t.tensor, ee * D * HF + jg * 256,
                                [[HF, 128], [128 * HF, 8], [1, 256]])),
                    reads=["@w1"], writes=[wg1n], kind="d")
                P.op("y", lambda e, ee=ee, jg=jg, wg2=wg2: e.dma_start(
                    out=wg2[:],
                    in_=bass.AP(w2t.tensor, ee * HF * D + jg * 256 * D,
                                [[D, 128], [128 * D, 2], [1, D]])),
                    reads=["@w2"], writes=[wg2n], kind="d")

            def w1_group(g):
                sl, tc = g // NTC, g % NTC
                ee, jg = sl // NJG, sl % NJG
                wg1, wg1n = w1g[sl % 2], f"w1g{sl % 2}"
                hb, hn = (g % 3) * 1024, f"h{g % 3}"
                for js in range(2):
                    ph, phn = p_mD[js], f"p_mD{js}"
                    for fc in range(8):
                        P.op("t", lambda e, js=js, fc=fc, tc=tc,
                             wg1=wg1, ph=ph: e.matmul(
                                 ph[:],
                                 wg1[:, fc * 256 + js * 128:
                                     fc * 256 + (js + 1) * 128],
                                 x1[:, fc * T + tc * 512:
                                    fc * T + (tc + 1) * 512],
                                 start=(fc == 0), stop=(fc == 7)),
                             reads=[wg1n, "x1"], writes=[phn])
                    bcoli = ee * (HF // 128) + jg * 2 + js
                    P.op("s", lambda e, js=js, bcoli=bcoli, ph=ph, hb=hb,
                         hn=hn: e.activation(
                             h_sb[:, hb + js * 512:hb + (js + 1) * 512],
                             ph[:, 0:512], AF.Gelu,
                             bias=b1c[:, bcoli:bcoli + 1], scale=1.0),
                         reads=[phn, "b1c"], writes=[hn])

            def w2_group(g):
                sl, tc = g // NTC, g % NTC
                ee = sl // NJG
                wg2, wg2n = w2g[sl % 2], f"w2g{sl % 2}"
                hb, hn = (g % 3) * 1024, f"h{g % 3}"
                for i in range(4):
                    tcb = tc * 4 + i
                    py, pyn = p_y[i % 2], f"p_y{i % 2}"
                    for half in range(2):
                        for js in range(2):
                            P.op("t", lambda e, i=i, half=half, js=js,
                                 wg2=wg2, py=py, hb=hb: e.matmul(
                                     py[:, half * 512:half * 512 + 512],
                                     h_sb[:, hb + js * 512 + i * 128:
                                          hb + js * 512 + (i + 1) * 128],
                                     wg2[:, js * D + half * 512:
                                         js * D + (half + 1) * 512],
                                     start=(js == 0), stop=(js == 1)),
                                 reads=[hn, wg2n], writes=[pyn])
                    P.op("v", lambda e, ee=ee, tcb=tcb, py=py:
                         e.scalar_tensor_tensor(
                             y_acc[:, tcb * D:(tcb + 1) * D],
                             py[:, 0:D],
                             comb[:, tcb * E + ee:tcb * E + ee + 1],
                             y_acc[:, tcb * D:(tcb + 1) * D],
                             ALU.mult, ALU.add),
                         reads=[pyn, "comb", "y_acc"],
                         writes=["y_acc"])

            slab_load(0)
            w1_group(0)
            for g in range(G):
                sl, tc = g // NTC, g % NTC
                if tc == 0 and sl + 1 < NSL:
                    slab_load(sl + 1)
                if g + 1 < G:
                    w1_group(g + 1)
                w2_group(g)

            for tcb in range(T128):
                P.op("v", lambda e, tcb=tcb: e.reduce_sum(
                    mcol[:], y_acc[:, tcb * D:(tcb + 1) * D],
                    axis=mybir.AxisListType.X),
                    reads=["y_acc"], writes=["mcol"])
                P.op("v", lambda e: e.tensor_scalar(
                    mcol[:], mcol[:], 1.0 / D, None, ALU.mult),
                    reads=["mcol"], writes=["mcol"])
                P.op("s", lambda e, tcb=tcb: e.square(
                    h_sb[:, 0:D], y_acc[:, tcb * D:(tcb + 1) * D]),
                    reads=["y_acc"], writes=["h0"])
                P.op("v", lambda e: e.reduce_sum(
                    rcol[:], h_sb[:, 0:D], axis=mybir.AxisListType.X),
                    reads=["h0"], writes=["rcol"])
                P.op("v", lambda e: e.tensor_scalar(
                    rcol[:], rcol[:], 1.0 / D, None, ALU.mult),
                    reads=["rcol"], writes=["rcol"])
                P.op("v", lambda e: e.tensor_mul(h_sb[:, 0:1], mcol[:],
                                                 mcol[:]),
                     reads=["mcol"], writes=["h0"])
                P.op("v", lambda e: e.tensor_sub(rcol[:], rcol[:],
                                                 h_sb[:, 0:1]),
                     reads=["rcol", "h0"], writes=["rcol"])
                P.op("s", lambda e: e.activation(rcol[:], rcol[:], AF.Ln,
                                                 bias=cst[:, 1:2], scale=1.0),
                     reads=["rcol", "cst"], writes=["rcol"])
                P.op("s", lambda e: e.activation(rcol[:], rcol[:], AF.Exp,
                                                 bias=0.0, scale=-0.5),
                     reads=["rcol"], writes=["rcol"])
                P.op("v", lambda e, tcb=tcb: e.tensor_scalar(
                    h_sb[:, D:2 * D], y_acc[:, tcb * D:(tcb + 1) * D],
                    mcol[:], rcol[:], ALU.subtract, ALU.mult),
                    reads=["y_acc", "mcol", "rcol"], writes=["h1"])
                P.op("y", lambda e, tcb=tcb: e.dma_start(
                    out=outt[tcb * 128:(tcb + 1) * 128, :],
                    in_=h_sb[:, D:2 * D]),
                    reads=["h1"], writes=["@out"], kind="d")

        P.final_waits()
        stats = P.stats()
        P.finalize(block)
    if want_stats:
        return nc, stats
    return nc


def _hslice(h_sb, js, i):
    return h_sb[:, js * 512 + i * 128:js * 512 + (i + 1) * 128]



_NC_CACHE = {}


def kernel(L0_attn_hidden_states, L0_attn_rel_embeddings, L0_attn_Wq,
           L0_attn_bq, L0_attn_Wk, L0_attn_bk, L0_attn_Wv, L0_attn_bv,
           L0_ffn_Wr, L0_ffn_W1, L0_ffn_b1, L0_ffn_W2, L0_ffn_b2):
    x = np.ascontiguousarray(np.asarray(L0_attn_hidden_states, dtype=np.float32))
    B, S, D = x.shape
    C = CFG(S)
    key = S
    if key not in _NC_CACHE:
        _NC_CACHE[key] = build(C)
    nc = _NC_CACHE[key]

    def f32(a):
        return np.ascontiguousarray(np.asarray(a, dtype=np.float32))

    com = {
        "wq": f32(L0_attn_Wq), "wk": f32(L0_attn_Wk), "wv": f32(L0_attn_Wv),
        "bq": f32(L0_attn_bq).reshape(1, D),
        "bk": f32(L0_attn_bk).reshape(1, D),
        "bv": f32(L0_attn_bv).reshape(1, D),
        "rel": f32(L0_attn_rel_embeddings),
        "wr": f32(L0_ffn_Wr), "w1": f32(L0_ffn_W1), "b1": f32(L0_ffn_b1),
        "w2": f32(L0_ffn_W2), "b2": f32(L0_ffn_b2),
    }
    in_maps = []
    for core in range(8):
        b, qi = core // 4, core % 4
        q0 = qi * C.T
        nb = np.zeros((C.NB, D), dtype=np.float32)
        glo, ghi = q0 - C.NPAD, q0 + C.T + C.NPAD
        slo, shi = max(0, glo), min(S, ghi)
        nb[slo - glo:shi - glo] = x[b, slo:shi]
        farH = x[b, 0:max(0, q0 - C.NPAD)]
        farL = x[b, min(S, q0 + C.T + C.NPAD):S]
        fb = np.zeros((C.FB, D), dtype=np.float32)
        fb[0:len(farH)] = farH
        fb[len(farH):len(farH) + len(farL)] = farL
        ksrc = np.concatenate([nb, fb], axis=0)
        m = dict(com)
        m["xin"] = np.ascontiguousarray(ksrc)
        in_maps.append(m)

    res = run_bass_kernel_spmd(nc, in_maps, core_ids=list(range(8)))
    outf = np.empty((B, S, D), dtype=np.float32)
    for core in range(8):
        b, qi = core // 4, core % 4
        outf[b, qi * C.T:(qi + 1) * C.T] = res.results[core]["out"]
    return outf
